# version 46
# baseline (speedup 1.0000x reference)
import contextlib
import numpy as np
import ml_dtypes
import concourse.bass as bass
import concourse.mybir as mybir
from concourse.bass_utils import run_bass_kernel_spmd

F32 = mybir.dt.float32
BF16 = mybir.dt.bfloat16
AF = mybir.ActivationFunctionType
ALU = mybir.AluOpType
AX = mybir.AxisListType

D = 1024
S = 4096
FH = 4096
NDC = 8
NHC = 32
VW = 1040
EPS = 1e-5
LNC = float(np.log(0.125 * 1.04))


def build_nc(p1_sc=8, p2_g=8, p2_h=16, p3_sc=8, debug=False):
    nc = bass.Bass("TRN2", target_bir_lowering=False)

    x_d = nc.dram_tensor("x", [S, D], F32, kind="ExternalInput")
    wqkvT_d = nc.dram_tensor("wqkvT", [D, 3 * D], F32, kind="ExternalInput")
    woT_d = nc.dram_tensor("woT", [D, D], F32, kind="ExternalInput")
    w1r_d = nc.dram_tensor("w1r", [NHC, 128, NDC * 128], F32, kind="ExternalInput")
    w2r_d = nc.dram_tensor("w2r", [NDC, 128, NHC * 128], F32, kind="ExternalInput")
    a1r1_d = nc.dram_tensor("a1r1", [128, NDC * 27], F32, kind="ExternalInput")
    a2r2_d = nc.dram_tensor("a2r2", [128, NHC * 27], F32, kind="ExternalInput")
    b1a_d = nc.dram_tensor("b1a", [24, FH], F32, kind="ExternalInput")
    b2a_d = nc.dram_tensor("b2a", [24, D], F32, kind="ExternalInput")
    b1c_d = nc.dram_tensor("b1c", [128, NHC], F32, kind="ExternalInput")
    b2c_d = nc.dram_tensor("b2c", [128, NDC], F32, kind="ExternalInput")
    idnb_d = nc.dram_tensor("idnb", [128, 128], BF16, kind="ExternalInput")
    idnf_d = nc.dram_tensor("idnf", [128, 128], F32, kind="ExternalInput")
    ones2_d = nc.dram_tensor("ones2", [128, 2], BF16, kind="ExternalInput")
    ones64r_d = nc.dram_tensor("ones64r", [1, 64], F32, kind="ExternalInput")
    ones128r_d = nc.dram_tensor("ones128r", [1, 128], F32, kind="ExternalInput")
    ones128c_d = nc.dram_tensor("ones128c", [128, 1], BF16, kind="ExternalInput")
    masks_d = nc.dram_tensor("masks", [128, 2048], BF16, kind="ExternalInput")

    y_d = nc.dram_tensor("y", [S, D], F32, kind="ExternalOutput")
    sk = "ExternalOutput" if debug else "Internal"
    xlnT_d = nc.dram_tensor("xlnT", [D, S], BF16, kind=sk)
    kT_d = nc.dram_tensor("kT", [D, S], BF16, kind=sk)
    x2_d = nc.dram_tensor("x2", [S, D], F32, kind=sk)
    w1bf_d = nc.dram_tensor("w1bf", [NHC, 128, NDC * 128], BF16, kind="Internal")
    w2bf_d = nc.dram_tensor("w2bf", [NDC, 128, NHC * 128], BF16, kind="Internal")

    EM = {k: [] for k in ("sync", "gpsimd", "tensor", "scalar", "vector")}
    SYNC_SEMS = ["c", "x0", "x1", "xlt0", "xlt1", "kv0", "kv1", "kv2",
                 "xg0", "xg1", "kh0", "kh1", "kh2", "kh3", "xi", "x2s",
                 "x2l0", "x2l1", "smc", "o"]
    GP_SEMS = ["wq", "wkv", "wo", "sm", "wc", "wcs", "w10", "w11", "w12", "w13",
               "w20", "w21", "w22", "w23"]
    CNT = {s: 0 for s in ["pe", "act", "dve"] + SYNC_SEMS + GP_SEMS}
    SEMH = {}

    def q(qn, fn):
        EM[qn].append(fn)

    WHW = {k: {} for k in EM}

    def W(qn, sem, val):
        if val is None or val <= 0:
            return
        if WHW[qn].get(sem, 0) >= val:
            return
        WHW[qn][sem] = val
        q(qn, lambda e, s=sem, v=val: e.wait_ge(SEMH[s], v))

    def WR(qn, res):
        W(qn, res[0], res[1])

    def WD(qn, sem):
        W(qn, sem, CNT[sem])

    def quiesce():
        for s in SYNC_SEMS:
            WD("sync", s)
        for s in GP_SEMS:
            WD("gpsimd", s)
            WD("sync", s)

    def OP(qn, build, inc=None, by=1):
        if inc is None:
            q(qn, lambda e, b=build: b(e))
            return None
        CNT[inc] += by
        q(qn, lambda e, b=build, s=inc, n=by: b(e).then_inc(SEMH[s], n))
        return CNT[inc]

    def DMA(qn, sem, out_ap, in_ap):
        return OP(qn, lambda e, o=out_ap, i=in_ap: e.dma_start(o, i), inc=sem, by=16)

    def MM(out_ap, lhsT, rhs, start, stop, inc=None):
        return OP("tensor",
                  lambda e, o=out_ap, l=lhsT, r=rhs, st=start, sp=stop:
                  e.matmul(o, l, r, start=st, stop=sp), inc=inc)

    def TRx(out_ap, in_ap, idn, inc=None):
        return OP("tensor",
                  lambda e, o=out_ap, i=in_ap, d=idn: e.transpose(o, i, d), inc=inc)

    def ACT(out_ap, in_ap, func, bias=0.0, scale=1.0):
        return OP("scalar",
                  lambda e, o=out_ap, i=in_ap, f=func, b=bias, s=scale:
                  e.activation(o, i, f, bias=b, scale=s), inc="act")

    def flush_block():
        with nc.Block() as blk:
            for name, dec in (("sync", blk.sync), ("gpsimd", blk.gpsimd),
                              ("tensor", blk.tensor), ("scalar", blk.scalar),
                              ("vector", blk.vector)):
                fns = EM[name]
                EM[name] = []
                if fns:
                    def body(eng, fns=fns):
                        for f in fns:
                            f(eng)
                    dec(body)

    NONE = ("pe", 0)

    with contextlib.ExitStack() as ES:
        for s in CNT:
            SEMH[s] = ES.enter_context(nc.semaphore(s))
        pAB = ES.enter_context(nc.psum_tensor("pAB", [128, 1024], F32))
        pCD = ES.enter_context(nc.psum_tensor("pCD", [128, 1024], F32))
        pE = ES.enter_context(nc.psum_tensor("pE", [128, 512], F32))
        pF = ES.enter_context(nc.psum_tensor("pF", [128, 512], F32))
        pG = ES.enter_context(nc.psum_tensor("pG", [128, 512], F32))
        pT = ES.enter_context(nc.psum_tensor("pT", [128, 1024], BF16))
        HS = [pAB[:, 0:512], pAB[:, 512:1024], pCD[:, 0:512], pCD[:, 512:1024]]
        half_free = [NONE, NONE, NONE, NONE]
        pE_free = NONE
        pF_free = NONE
        pG_free = NONE
        pT_free = NONE

        idnb = ES.enter_context(nc.sbuf_tensor("idnb_s", [128, 128], BF16))
        idnf = ES.enter_context(nc.sbuf_tensor("idnf_s", [128, 128], F32))
        ones2 = ES.enter_context(nc.sbuf_tensor("ones2_s", [128, 2], BF16))
        ones64r = ES.enter_context(nc.sbuf_tensor("ones64r_s", [1, 64], F32))
        ones128r = ES.enter_context(nc.sbuf_tensor("ones128r_s", [1, 128], F32))
        ones128c = ES.enter_context(nc.sbuf_tensor("ones128c_s", [128, 1], BF16))
        stat = ES.enter_context(nc.sbuf_tensor("stat", [128, 32], F32))
        srow = ES.enter_context(nc.sbuf_tensor("srow", [1, 16], F32))
        rowb = ES.enter_context(nc.sbuf_tensor("rowb", [1, 1024], F32))
        wqTb = ES.enter_context(nc.sbuf_tensor("wqTb", [128, NDC * 1024], BF16))
        knmc = ES.enter_context(nc.sbuf_tensor("knmc", [2, 64], F32))

        C_S1, C_S2, C_NM, C_VE, C_M2, C_VAR, C_RSTD, C_NMR, C_NEGC, C_NGMX, C_GSUM, C_GSC = range(12)
        KMX_COL = 16
        QMX_COL = 17
        R_MK2, R_LNMK, R_MQ2, R_LNQ, R_LSUM, R_CG, R_NEGC1 = range(7)

        def statc(i):
            return stat[:, i:i + 1]

        def srowc(i):
            return srow[0:1, i:i + 1]

        xsq_user = [0]

        def ln_chain(qv, qa, xi, xsq_t, out_ap):
            OP(qv, lambda e, i=xi: e.reduce_sum(statc(C_S1), i, AX.X), inc="dve")
            W(qa, "dve", xsq_user[0])
            a0 = ACT(xsq_t[:, :], xi, AF.Square)
            W(qv, "act", a0)
            d1 = OP(qv, lambda e, t=xsq_t: e.reduce_sum(statc(C_S2), t[:, :], AX.X), inc="dve")
            xsq_user[0] = d1
            W(qv, "dve", d1)
            OP(qv, lambda e: e.tensor_scalar(statc(C_NM), statc(C_S1), -1.0 / D, None, op0=ALU.mult), inc="dve")
            d1b = OP(qv, lambda e: e.tensor_scalar(statc(C_VE), statc(C_S2), 1.0 / D, EPS, op0=ALU.mult, op1=ALU.add), inc="dve")
            W(qv, "dve", d1b)
            d2 = OP(qv, lambda e: e.tensor_mul(statc(C_M2), statc(C_NM), statc(C_NM)), inc="dve")
            W(qv, "dve", d2)
            d3 = OP(qv, lambda e: e.tensor_sub(statc(C_VAR), statc(C_VE), statc(C_M2)), inc="dve")
            W(qa, "dve", d3)
            a1 = ACT(statc(C_VAR), statc(C_VAR), AF.Ln)
            W(qa, "act", a1)
            a2 = ACT(statc(C_RSTD), statc(C_VAR), AF.Exp, scale=-0.5)
            W(qv, "act", a2)
            d4 = OP(qv, lambda e: e.tensor_mul(statc(C_NMR), statc(C_NM), statc(C_RSTD)), inc="dve")
            W(qv, "dve", d4)
            dout = OP(qv, lambda e, i=xi, o=out_ap:
                      e.tensor_scalar(o, i, statc(C_RSTD), statc(C_NMR), op0=ALU.mult, op1=ALU.add), inc="dve")
            return dout, a0

        S12 = contextlib.ExitStack()
        V = S12.enter_context(nc.sbuf_tensor("V", [128, 32 * VW], BF16))
        with contextlib.ExitStack() as S1:
            wkvTb = S1.enter_context(nc.sbuf_tensor("wkvTb", [128, NDC * 2048], BF16))
            xin1 = S1.enter_context(nc.sbuf_tensor("xin1", [128, 2 * 4096], F32))
            xsq = S1.enter_context(nc.sbuf_tensor("xsq", [128, 1024], F32))
            xlt = S1.enter_context(nc.sbuf_tensor("xlt", [128, 4096], BF16))
            xlT = S1.enter_context(nc.sbuf_tensor("xlT", [128, 2 * 4096], BF16))
            kvb = S1.enter_context(nc.sbuf_tensor("kvb", [128, 7 * 512], BF16))
            wstg = S1.enter_context(nc.sbuf_tensor("wstg", [128, 2 * 1024], BF16))
            ksq = S1.enter_context(nc.sbuf_tensor("ksq", [128, 512], BF16))

            DMA("sync", "c", idnb.ap(), idnb_d.ap())
            DMA("sync", "c", idnf.ap(), idnf_d.ap())
            DMA("sync", "c", ones2.ap(), ones2_d.ap())
            DMA("sync", "c", ones64r.ap(), ones64r_d.ap())
            DMA("sync", "c", ones128r.ap(), ones128r_d.ap())
            DMA("sync", "c", ones128c.ap(), ones128c_d.ap())
            DMA("gpsimd", "wq",
                bass.AP(wqTb, 0, [[NDC * 1024, 128], [1024, NDC], [1, 1024]]),
                bass.AP(wqkvT_d, 0, [[3 * D, 128], [128 * 3 * D, NDC], [1, 1024]]))
            DMA("gpsimd", "wkv",
                bass.AP(wkvTb, 0, [[NDC * 2048, 128], [2048, NDC], [1, 2048]]),
                bass.AP(wqkvT_d, 1024, [[3 * D, 128], [128 * 3 * D, NDC], [1, 2048]]))
            for i in range(64):
                wsrc, wdst = (w1r_d, w1bf_d) if i < 32 else (w2r_d, w2bf_d)
                off = (i % 32) * 128 * 1024
                slot = i % 2
                if i >= 2:
                    W("gpsimd", "wcs", 16 * (i - 1))
                ld = DMA("gpsimd", "wc",
                         bass.AP(wstg, slot * 1024, [[2 * 1024, 128], [1, 1024]]),
                         bass.AP(wsrc, off, [[1024, 128], [1, 1024]]))
                W("gpsimd", "wc", ld)
                DMA("gpsimd", "wcs",
                    bass.AP(wdst, off, [[1024, 128], [1, 1024]]),
                    bass.AP(wstg, slot * 1024, [[2 * 1024, 128], [1, 1024]]))
            for i in range(32):
                OP("vector", lambda e, a=bass.AP(V, i * VW + 64, [[32 * VW, 128], [65, 16], [1, 1]]):
                   e.memset(a, 1.0), inc="dve")

            WD("tensor", "c")

            xin_user = [NONE, NONE]
            xin_user_a = [0, 0]
            kvb_sq = [0, 0, 0, 0]
            xlT_user = [0, 0]
            kvb_user = [NONE] * 7
            ksq_user = 0

            def p1_load_x(sc):
                slot = sc % 2
                WR("sync", xin_user[slot])
                W("sync", "act", xin_user_a[slot])
                DMA("sync", "x%d" % slot,
                    bass.AP(xin1, slot * 4096, [[2 * 4096, 128], [1024, 4], [1, 1024]]),
                    bass.AP(x_d, sc * 512 * D, [[D, 128], [128 * D, 4], [1, 1024]]))

            p1_load_x(0)
            if p1_sc > 1:
                p1_load_x(1)

            def p1_transposes(sc, dxls):
                nonlocal pT_free
                slot = sc % 2
                for sub in range(4):
                    W("tensor", "dve", dxls[sub])
                    WR("tensor", pT_free)
                    tr = None
                    for dc in range(NDC):
                        tr = TRx(pT[:, dc * 128:(dc + 1) * 128],
                                 xlt[:, sub * 1024 + dc * 128: sub * 1024 + dc * 128 + 128],
                                 idnb[:, :], inc="pe" if dc == 7 else None)
                    W("vector", "pe", tr)
                    if sub == 0:
                        WD("vector", "xlt%d" % slot)
                        W("vector", "pe", xlT_user[slot])
                    ev = OP("vector", lambda e,
                            i=bass.AP(pT, 0, [[1024, 128], [128, NDC], [1, 128]]),
                            o=bass.AP(xlT, slot * 4096 + sub * 128, [[2 * 4096, 128], [512, NDC], [1, 128]]):
                            e.tensor_copy(o, i), inc="dve")
                    pT_free = ("dve", ev)
                dve_xlT = ev
                W("sync", "dve", dve_xlT)
                DMA("sync", "xlt%d" % slot,
                    bass.AP(xlnT_d, sc * 512, [[S, 128], [128 * S, NDC], [1, 512]]),
                    bass.AP(xlT, slot * 4096, [[2 * 4096, 128], [512, NDC], [1, 512]]))
                return dve_xlT

            dxls_pend = {}

            def p1_ln_sub(sc, sub):
                slot = sc % 2
                if sub == 0:
                    WD("vector", "x%d" % slot)
                    WD("scalar", "x%d" % slot)
                    dxls_pend[sc] = []
                xi = bass.AP(xin1, slot * 4096 + sub * 1024, [[2 * 4096, 128], [1, 1024]])
                dxl, asq = ln_chain("vector", "scalar", xi, xsq,
                                    xlt[:, sub * 1024:(sub + 1) * 1024])
                dxls_pend[sc].append(dxl)
                if sub == 3:
                    xin_user[slot] = ("dve", dxl)
                    xin_user_a[slot] = asq

            def p1_kvproj(sc, dve_xlT, ln_next):
                nonlocal ksq_user, pE_free, pT_free
                slot = sc % 2
                WD("tensor", "wkv")
                for fc in range(16):
                    bs = fc % 4
                    WR("tensor", half_free[bs])
                    if fc == 0:
                        W("tensor", "dve", dve_xlT)
                    mmv = None
                    for dc in range(NDC):
                        mmv = MM(HS[bs], wkvTb[:, dc * 2048 + fc * 128: dc * 2048 + fc * 128 + 128],
                                 bass.AP(xlT, slot * 4096 + dc * 512, [[2 * 4096, 128], [1, 512]]),
                                 start=(dc == 0), stop=(dc == 7), inc="pe" if dc == 7 else None)
                    kslot = fc % 4 if fc < 8 else 4 + fc % 3
                    W("scalar", "pe", mmv)
                    WD("scalar", "kv%d" % (kslot % 3))
                    WR("scalar", kvb_user[kslot])
                    if fc < 8:
                        W("scalar", "dve", kvb_sq[kslot])
                    evk = ACT(kvb[:, kslot * 512:(kslot + 1) * 512], HS[bs], AF.Copy)
                    half_free[bs] = ("act", evk)
                    if fc < 8:
                        W("scalar", "act", evk)
                        DMA("scalar", "kv%d" % (kslot % 3),
                            bass.AP(kT_d, fc * 128 * S + sc * 512, [[S, 128], [1, 512]]),
                            kvb[:, kslot * 512:(kslot + 1) * 512])
                        W("vector", "act", evk)
                        W("vector", "pe", ksq_user)
                        aq = OP("vector", lambda e, s=kvb[:, kslot * 512:(kslot + 1) * 512]:
                                e.tensor_mul(ksq[:, :], s, s), inc="dve")
                        kvb_sq[kslot] = aq
                        W("tensor", "dve", aq)
                        WR("tensor", pE_free)
                        nm = MM(pE[0:2, :], ones2[:, :], ksq[:, :], start=True, stop=True, inc="pe")
                        ksq_user = nm
                        W("vector", "pe", nm)
                        col = sc * 8 + fc
                        dmx = OP("vector", lambda e, c=col:
                                 e.reduce_max(knmc[0:2, c:c + 1], pE[0:2, :], AX.X), inc="dve")
                        pE_free = ("dve", dmx)
                        kvb_user[kslot] = ("kv%d" % (kslot % 3), CNT["kv%d" % (kslot % 3)])
                    else:
                        vfc = fc - 8
                        he = 2 * vfc
                        W("tensor", "act", evk)
                        WR("tensor", pT_free)
                        tr = None
                        for sub in range(4):
                            tr = TRx(pT[:, sub * 128:(sub + 1) * 128],
                                     kvb[:, kslot * 512 + sub * 128: kslot * 512 + sub * 128 + 128],
                                     idnb[:, :], inc="pe" if sub == 3 else None)
                        kvb_user[kslot] = ("pe", tr)
                        W("scalar", "pe", tr)
                        OP("scalar", lambda e,
                           i=bass.AP(pT, 0, [[1024, 128], [128, 4], [1, 64]]),
                           o=bass.AP(V, (sc * 4) * VW + he * 65, [[32 * VW, 128], [VW, 4], [1, 64]]):
                           e.activation(o, i, AF.Copy), inc="act")
                        ev2 = OP("scalar", lambda e,
                                 i=bass.AP(pT, 64, [[1024, 128], [128, 4], [1, 64]]),
                                 o=bass.AP(V, (sc * 4) * VW + (he + 1) * 65, [[32 * VW, 128], [VW, 4], [1, 64]]):
                                 e.activation(o, i, AF.Copy), inc="act")
                        pT_free = ("act", ev2)
                    if ln_next is not None and fc in (9, 11, 13, 15):
                        p1_ln_sub(ln_next, (fc - 9) // 2)
                xlT_user[slot] = CNT["pe"]

            for sub in range(4):
                p1_ln_sub(0, sub)
            for sc in range(p1_sc):
                if sc + 2 < p1_sc:
                    p1_load_x(sc + 2)
                dve_xlT = p1_transposes(sc, dxls_pend.pop(sc))
                p1_kvproj(sc, dve_xlT, sc + 1 if sc + 1 < p1_sc else None)

            W("vector", "dve", CNT["dve"])
            d5 = OP("vector", lambda e, n=8 * p1_sc:
                    e.reduce_max(stat[0:2, KMX_COL:KMX_COL + 1], knmc[0:2, 0:n], AX.X), inc="dve")
            W("tensor", "dve", d5)
            WR("tensor", pE_free)
            t5 = TRx(pE[0:1, 0:2], stat[0:2, KMX_COL:KMX_COL + 1], idnf[0:2, 0:2], inc="pe")
            W("vector", "pe", t5)
            d6 = OP("vector", lambda e: e.reduce_max(srowc(R_MK2), pE[0:1, 0:2], AX.X), inc="dve")
            pE_free = ("dve", d6)
            W("scalar", "dve", d6)
            ACT(srowc(R_LNMK), srowc(R_MK2), AF.Ln)

            quiesce()
            flush_block()

        with contextlib.ExitStack() as S2:
            woTb = S2.enter_context(nc.sbuf_tensor("woTb", [128, NDC * 1024], BF16))
            masks2 = S2.enter_context(nc.sbuf_tensor("masks2", [128, 2048], BF16))
            xlnTg = S2.enter_context(nc.sbuf_tensor("xlnTg", [128, 2 * 4096], BF16))
            qTg = S2.enter_context(nc.sbuf_tensor("qTg", [128, 8 * 512], BF16))
            qsq = S2.enter_context(nc.sbuf_tensor("qsq", [128, 512], BF16))
            Kh = S2.enter_context(nc.sbuf_tensor("Kh", [128, 4096], BF16))
            probsT = S2.enter_context(nc.sbuf_tensor("probsT", [128, 4 * 1024], BF16))
            headsT = S2.enter_context(nc.sbuf_tensor("headsT", [128, 2 * 8 * 512], BF16))
            rb = S2.enter_context(nc.sbuf_tensor("rb", [128, 512], F32))
            rb2 = S2.enter_context(nc.sbuf_tensor("rb2", [128, 512], F32))
            aob = S2.enter_context(nc.sbuf_tensor("aob", [128, 2 * 512], F32))
            xing = S2.enter_context(nc.sbuf_tensor("xing", [128, 4096], F32))
            x2b = S2.enter_context(nc.sbuf_tensor("x2b", [128, 4096], F32))

            DMA("gpsimd", "wo",
                bass.AP(woTb, 0, [[NDC * 1024, 128], [1024, NDC], [1, 1024]]),
                bass.AP(woT_d, 0, [[D, 128], [128 * D, NDC], [1, 1024]]))
            DMA("sync", "c", masks2.ap(), masks_d.ap())
            WD("vector", "c")

            xlnTg_user = [0, 0]
            Kh_user = [0, 0, 0, 0]
            headsT_user = [0, 0]
            pend_wo = []
            xing_user = 0
            qTg_user = 0
            qsq_user = [0, 0]
            norm_done = [0, 0]
            probs_user = [0, 0, 0, 0]
            sq2 = [0]
            rowb_user = 0
            rb_user = 0
            ao_user = [0, 0]

            def p2_load_xlnTg(g):
                slot = g % 2
                W("sync", "pe", xlnTg_user[slot])
                DMA("sync", "xg%d" % slot,
                    bass.AP(xlnTg, slot * 4096, [[2 * 4096, 128], [512, NDC], [1, 512]]),
                    bass.AP(xlnT_d, g * 512, [[S, 128], [128 * S, NDC], [1, 512]]))

            def p2_load_Kh(g, h):
                kmax = 512 * (g + 1)
                if g < 4:
                    slot = h % 4
                    off = (h % 2) * 64 * 4096 + ((h // 2) % 2) * 2048
                    W("sync", "pe", Kh_user[slot])
                    DMA("sync", "kh%d" % slot,
                        bass.AP(Kh, off, [[4096, 64], [1, kmax]]),
                        bass.AP(kT_d, h * 64 * S, [[S, 64], [1, kmax]]))
                else:
                    slot = h % 2
                    W("sync", "pe", max(Kh_user[slot], Kh_user[slot + 2]))
                    DMA("sync", "kh%d" % slot,
                        bass.AP(Kh, slot * 64 * 4096, [[4096, 64], [1, kmax]]),
                        bass.AP(kT_d, h * 64 * S, [[S, 64], [1, kmax]]))

            def emit_qproj(g):
                nonlocal pG_free, qn_last
                gslot = g % 2
                WD("tensor", "wq")
                WD("tensor", "xg%d" % gslot)
                pend_q = []

                def emit_qn(is_last):
                    nonlocal qn_last
                    if not pend_q:
                        return
                    m2_, first_ = pend_q.pop()
                    W("tensor", "dve", m2_)
                    if first_:
                        WR("tensor", pG_free)
                    qn_last = MM(pG[0:1, :], ones128c[:, 0:1], qsq[:, :],
                                 start=first_, stop=is_last, inc="pe")
                    qsq_user[0] = qn_last

                for fc in range(8):
                    bs = fc % 4
                    WR("tensor", half_free[bs])
                    mmq = None
                    for dc in range(NDC):
                        mmq = MM(HS[bs], wqTb[:, dc * 1024 + fc * 128: dc * 1024 + fc * 128 + 128],
                                 bass.AP(xlnTg, gslot * 4096 + dc * 512, [[2 * 4096, 128], [1, 512]]),
                                 start=(dc == 0), stop=(dc == 7), inc="pe" if dc == 7 else None)
                    W("vector", "pe", mmq)
                    if fc == 0:
                        W("vector", "pe", qTg_user)
                    q1 = qTg[:, fc * 512:fc * 512 + 512]
                    c2 = OP("vector", lambda e, o=q1, b=HS[bs]: e.tensor_copy(o, b), inc="dve")
                    half_free[bs] = ("dve", c2)
                    emit_qn(False)
                    W("vector", "dve", c2)
                    W("vector", "pe", qsq_user[0])
                    m2 = OP("vector", lambda e, i=q1: e.tensor_mul(qsq[:, :], i, i), inc="dve")
                    pend_q.append((m2, fc == 0))
                emit_qn(True)
                xlnTg_user[gslot] = CNT["pe"]
                if g + 2 < p2_g:
                    p2_load_xlnTg(g + 2)

                W("vector", "pe", qn_last)
                d8 = OP("vector", lambda e: e.reduce_max(srowc(R_MQ2), pG[0:1, 0:512], AX.X), inc="dve")
                W("scalar", "dve", d8)
                a3 = ACT(srowc(R_LNQ), srowc(R_MQ2), AF.Ln)
                W("vector", "act", a3)
                d9 = OP("vector", lambda e: e.tensor_add(srowc(R_LSUM), srowc(R_LNQ), srowc(R_LNMK)), inc="dve")
                W("scalar", "dve", d9)
                a4 = ACT(srowc(R_CG), srowc(R_LSUM), AF.Exp, scale=0.5, bias=np.float32(LNC))
                W("vector", "act", a4)
                d10 = OP("vector", lambda e: e.tensor_scalar(srowc(R_NEGC1), srowc(R_CG), -1.0, None, op0=ALU.mult), inc="dve")
                W("tensor", "dve", d10)
                t9 = MM(pG[0:128, 3:4], ones128r[0:1, :], srowc(R_NEGC1), start=True, stop=True, inc="pe")
                W("scalar", "pe", t9)
                a5 = ACT(statc(C_NEGC), pG[0:128, 3:4], AF.Copy)
                pG_free = ("act", a5)
                W("scalar", "act", a5)

            qn_last = 0
            p2_load_xlnTg(0)
            if p2_g > 1:
                p2_load_xlnTg(1)
            emit_qproj(0)
            for h0 in range(4):
                p2_load_Kh(0, h0)

            for g in range(p2_g):
                if g == 0:
                    DMA("sync", "xi",
                        bass.AP(xing, 0, [[4096, 128], [1024, 4], [1, 1024]]),
                        bass.AP(x_d, 0, [[D, 128], [128 * D, 4], [1, 1024]]))
                pairs = 2 * g + 2
                pending_norm = []

                def emit_norm():
                    nonlocal pG_free, rowb_user, rb_user
                    if not pending_norm:
                        return
                    ha, pva, pvla = pending_norm.pop(0)
                    hb, pvb2, pvlb = pending_norm.pop(0)
                    W("vector", "pe", max(pvla, pvlb))
                    W("vector", "pe", rowb_user)
                    OP("vector", lambda e: e.tensor_copy(rowb[0:1, 0:512], pva[64:65, :]), inc="dve")
                    cbs = OP("vector", lambda e: e.tensor_copy(rowb[0:1, 512:1024], pvb2[64:65, :]), inc="dve")
                    W("tensor", "dve", cbs)
                    WR("tensor", pG_free)
                    MM(pG[0:64, :], ones64r[0:1, :], rowb[0:1, 0:512],
                       start=True, stop=True)
                    t10 = MM(pG[64:128, :], ones64r[0:1, :], rowb[0:1, 512:1024],
                             start=True, stop=True, inc="pe")
                    rowb_user = t10
                    W("vector", "pe", t10)
                    W("vector", "dve", rb_user)
                    cb = OP("vector", lambda e: e.tensor_copy(rb[:, :], pG[:, :]), inc="dve")
                    pG_free = ("dve", cb)
                    W("vector", "dve", cb)
                    rcp = OP("vector", lambda e: e.reciprocal(rb2[:, :], rb[:, :]), inc="dve")
                    W("vector", "dve", rcp)
                    W("vector", "pe", headsT_user[g % 2])
                    hoff = (g % 2) * 4096
                    mul1 = OP("vector", lambda e, h2=ha, pv=pva, ho=hoff:
                              e.tensor_mul(headsT[0:64, ho + (h2 // 2) * 512: ho + (h2 // 2) * 512 + 512],
                                           pv[0:64, :], rb2[0:64, :]), inc="dve")
                    rb_user = OP("vector", lambda e, h2=hb, pv=pvb2, ho=hoff:
                                 e.tensor_mul(headsT[64:128, ho + (h2 // 2) * 512: ho + (h2 // 2) * 512 + 512],
                                              pv[0:64, :], rb2[64:128, :]), inc="dve")
                    norm_done[0] = mul1
                    norm_done[1] = rb_user

                for hp in range(p2_h // 2):
                    h1, h2 = 2 * hp, 2 * hp + 1
                    ks0 = (2 * hp) % 4 if g < 4 else 0
                    chalf = (hp % 2) * 2048 if g < 4 else 0
                    WD("tensor", "kh%d" % ks0)
                    WD("tensor", "kh%d" % (ks0 + 1))
                    pend = []
                    pv_done = {}

                    def emit_pv():
                        if not pend:
                            return
                        hh, pvb_, ps_, pi_, gate0_, gate1_ = pend.pop(0)
                        if pi_ == 0:
                            W("tensor", "dve", norm_done[hh % 2])
                        mpv_ = None
                        for half in range(2):
                            WR("tensor", gate0_ if half == 0 else gate1_)
                            kc = 2 * pi_ + half
                            q0 = max(0, (kc - 4 * g)) * 128
                            if pi_ == 0 and half == 0:
                                q0 = 0
                            mpv_ = MM(pvb_[0:65, q0:],
                                      V[:, kc * VW + hh * 65: kc * VW + hh * 65 + 65],
                                      probsT[:, ps_ * 1024 + half * 512 + q0: ps_ * 1024 + half * 512 + 512],
                                      start=(pi_ == 0 and half == 0),
                                      stop=(pi_ == pairs - 1 and half == 1),
                                      inc="pe" if half == 1 else None)
                        pv_done[(hh, pi_)] = mpv_
                        probs_user[ps_] = mpv_

                    for pi in range(pairs):
                        for sub, (hh, pvb_) in enumerate(((h1, pE), (h2, pF))):
                            b3 = sq2[0] % 2
                            ps = sq2[0] % 4
                            sq2[0] += 1
                            pbank = pAB if b3 == 0 else pCD
                            mms = []
                            for half in range(2):
                                WR("tensor", half_free[2 * b3 + half])
                                kc = 2 * pi + half
                                q0 = max(0, (kc - 4 * g)) * 128
                                mms.append(MM(pbank[:, half * 512 + q0:half * 512 + 512],
                                              Kh[sub * 64: sub * 64 + 64,
                                                 chalf + kc * 128: chalf + kc * 128 + 128],
                                              qTg[sub * 64: sub * 64 + 64, hp * 512 + q0:hp * 512 + 512],
                                              start=True, stop=True, inc="pe"))
                            if sub == 1 and pi > 0:
                                emit_pv()
                                emit_pv()
                            gates = []
                            for half in range(2):
                                W("scalar", "pe", mms[half])
                                W("scalar", "pe", probs_user[ps])
                                ex = ACT(probsT[:, ps * 1024 + half * 512: ps * 1024 + half * 512 + 512],
                                         pbank[:, half * 512:half * 512 + 512],
                                         AF.Exp, bias=statc(C_NEGC), scale=0.125)
                                half_free[2 * b3 + half] = ("act", ex)
                                gate = ("act", ex)
                                if pi >= 2 * g:
                                    o = pi - 2 * g
                                    W("vector", "act", ex)
                                    gv = OP("vector", lambda e, p_=ps, oo=o, hf=half:
                                            e.tensor_mul(probsT[:, p_ * 1024 + hf * 512:p_ * 1024 + hf * 512 + 512],
                                                         probsT[:, p_ * 1024 + hf * 512:p_ * 1024 + hf * 512 + 512],
                                                         masks2[:, oo * 1024 + hf * 512:oo * 1024 + hf * 512 + 512]), inc="dve")
                                    gate = ("dve", gv)
                                gates.append(gate)
                            pend.append((hh, pvb_, ps, pi, gates[0], gates[1]))
                        if pi == 0:
                            emit_norm()
                    emit_pv()
                    emit_pv()
                    Kh_user[ks0] = pv_done[(h1, pairs - 1)]
                    Kh_user[ks0 + 1] = pv_done[(h2, pairs - 1)]
                    if g < 4:
                        if 2 * hp + 5 < p2_h:
                            p2_load_Kh(g, 2 * hp + 4)
                            p2_load_Kh(g, 2 * hp + 5)
                    elif hp + 1 < p2_h // 2:
                        p2_load_Kh(g, h2 + 1)
                        p2_load_Kh(g, h2 + 2)
                    pending_norm.append((h1, pE, pv_done[(h1, pairs - 1)]))
                    pending_norm.append((h2, pF, pv_done[(h2, pairs - 1)]))
                    if pend_wo:
                        pend_wo.pop(0)()
                emit_norm()
                if pend_wo:
                    pend_wo.pop(0)()
                qTg_user = CNT["pe"]

                if g + 1 < p2_g:
                    emit_qproj(g + 1)
                    for h0 in range(4 if g + 1 < 4 else 2):
                        p2_load_Kh(g + 1, h0)

                def mk_wo_closures(gw, nd0, nd1):
                    evs = {}

                    def stageA(dc):
                        hoff = (gw % 2) * 4096
                        bs = 2 * (dc % 2)
                        if dc == 0:
                            WD("tensor", "wo")
                        W("tensor", "dve", nd0)
                        W("tensor", "dve", nd1)
                        WR("tensor", half_free[bs])
                        mwo = None
                        for ec in range(NDC):
                            mwo = MM(HS[bs], woTb[:, ec * 1024 + dc * 128: ec * 1024 + dc * 128 + 128],
                                     headsT[:, hoff + ec * 512: hoff + (ec + 1) * 512],
                                     start=(ec == 0), stop=(ec == 7), inc="pe" if ec == 7 else None)
                        headsT_user[gw % 2] = mwo
                        W("vector", "pe", mwo)
                        W("vector", "pe", ao_user[dc % 2])
                        ev2 = OP("vector", lambda e, sl=dc % 2, b=HS[bs]:
                                 e.tensor_copy(aob[:, sl * 512: sl * 512 + 512], b), inc="dve")
                        half_free[bs] = ("dve", ev2)
                        evs[dc] = ev2

                    def stageB(dc):
                        nonlocal pG_free, xing_user
                        W("tensor", "dve", evs[dc])
                        WR("tensor", pG_free)
                        ttr = None
                        for j in range(4):
                            ttr = TRx(pG[:, j * 128:(j + 1) * 128],
                                      aob[:, (dc % 2) * 512 + j * 128: (dc % 2) * 512 + j * 128 + 128],
                                      idnf[:, :], inc="pe" if j == 3 else None)
                        ao_user[dc % 2] = ttr
                        W("vector", "pe", ttr)
                        if dc == 0:
                            WD("vector", "xi")
                            WD("vector", "x2s")
                        adds_done = OP("vector", lambda e, d=dc:
                                       e.tensor_add(bass.AP(x2b, d * 128, [[4096, 128], [1024, 4], [1, 128]]),
                                                    bass.AP(pG, 0, [[512, 128], [128, 4], [1, 128]]),
                                                    bass.AP(xing, d * 128, [[4096, 128], [1024, 4], [1, 128]])),
                                       inc="dve")
                        pG_free = ("dve", adds_done)
                        if dc == NDC - 1:
                            xing_user = adds_done
                            W("sync", "dve", adds_done)
                            DMA("sync", "x2s",
                                bass.AP(x2_d, gw * 512 * D, [[D, 128], [128 * D, 4], [1, 1024]]),
                                bass.AP(x2b, 0, [[4096, 128], [1024, 4], [1, 1024]]))
                            if gw + 1 < p2_g:
                                WD("sync", "x2s")
                                DMA("sync", "xi",
                                    bass.AP(xing, 0, [[4096, 128], [1024, 4], [1, 1024]]),
                                    bass.AP(x_d, (gw + 1) * 512 * D, [[D, 128], [128 * D, 4], [1, 1024]]))

                    def mk(dc):
                        def c():
                            if dc > 0:
                                stageB(dc - 1)
                            stageA(dc)
                        return c
                    return [mk(dc) for dc in range(NDC)] + [lambda: stageB(NDC - 1)]

                pend_wo.extend(mk_wo_closures(g, norm_done[0], norm_done[1]))
                if g + 1 == p2_g:
                    while pend_wo:
                        pend_wo.pop(0)()

            quiesce()
            flush_block()
        S12.close()

        with contextlib.ExitStack() as S3:
            x2in = S3.enter_context(nc.sbuf_tensor("x2in", [128, 2 * 4096], F32))
            xsq3 = S3.enter_context(nc.sbuf_tensor("xsq3", [128, 1024], F32))
            xlt3 = S3.enter_context(nc.sbuf_tensor("xlt3", [128, 4096], BF16))
            xl2T = S3.enter_context(nc.sbuf_tensor("xl2T", [128, 2 * 4096], BF16))
            hT = S3.enter_context(nc.sbuf_tensor("hT", [128, NHC * 512], BF16))
            w1b = S3.enter_context(nc.sbuf_tensor("w1b", [128, 4 * 1024], BF16))
            w2b = S3.enter_context(nc.sbuf_tensor("w2b", [128, 4 * 4096], BF16))
            a1r1b = S3.enter_context(nc.sbuf_tensor("a1r1b", [128, NDC * 27], BF16))
            a2r2b = S3.enter_context(nc.sbuf_tensor("a2r2b", [128, NHC * 27], BF16))
            b1ab = S3.enter_context(nc.sbuf_tensor("b1ab", [24, FH], BF16))
            b2ab = S3.enter_context(nc.sbuf_tensor("b2ab", [24, D], BF16))
            b1cs = S3.enter_context(nc.sbuf_tensor("b1cs", [128, NHC], F32))
            b2cs = S3.enter_context(nc.sbuf_tensor("b2cs", [128, NDC], F32))
            zg1s = S3.enter_context(nc.sbuf_tensor("zg1s", [27, 512], F32))
            zts = S3.enter_context(nc.sbuf_tensor("zts", [128, 108], F32))
            gex = S3.enter_context(nc.sbuf_tensor("gex", [128, 12], F32))
            zgt = S3.enter_context(nc.sbuf_tensor("zgt", [128, 96], BF16))
            zg1T = S3.enter_context(nc.sbuf_tensor("zg1T", [24, 512], BF16))
            zg2T = S3.enter_context(nc.sbuf_tensor("zg2T", [24, 512], BF16))
            aob3 = S3.enter_context(nc.sbuf_tensor("aob3", [128, 2 * 512], F32))
            outb = S3.enter_context(nc.sbuf_tensor("outb", [128, 4096], F32))

            DMA("gpsimd", "sm", a1r1b.ap(), a1r1_d.ap())
            DMA("gpsimd", "sm", a2r2b.ap(), a2r2_d.ap())
            DMA("gpsimd", "sm", b1ab.ap(), b1a_d.ap())
            DMA("gpsimd", "sm", b2ab.ap(), b2a_d.ap())
            DMA("sync", "smc", b1cs.ap(), b1c_d.ap())
            DMA("sync", "smc", b2cs.ap(), b2c_d.ap())
            WD("scalar", "smc")
            WD("vector", "smc")

            x2_user = [NONE, NONE]
            x2_user_a = [0, 0]
            w1_user = [0, 0, 0, 0]
            w2_user = [0, 0, 0, 0]
            hT_user = 0
            xl2T_user = [0, 0]
            relu_done = [0, 0]
            obank_free = [0, 0]
            ao3_user = [0, 0]
            zgt_seq = 0

            def p3_load_x2(sc):
                slot = sc % 2
                WR("sync", x2_user[slot])
                W("sync", "act", x2_user_a[slot])
                DMA("sync", "x2l%d" % slot,
                    bass.AP(x2in, slot * 4096, [[2 * 4096, 128], [1024, 4], [1, 1024]]),
                    bass.AP(x2_d, sc * 512 * D, [[D, 128], [128 * D, 4], [1, 1024]]))

            def p3_load_w1(flat):
                sc_, hc_ = divmod(flat, NHC)
                if sc_ >= p3_sc:
                    return
                slot = flat % 4
                W("gpsimd", "pe", w1_user[slot])
                DMA("gpsimd", "w1%d" % slot,
                    bass.AP(w1b, slot * 1024, [[4 * 1024, 128], [1, 1024]]),
                    bass.AP(w1bf_d, hc_ * 128 * 1024, [[1024, 128], [1, 1024]]))

            def p3_load_w2(f):
                if f >= p3_sc * NDC:
                    return
                dc_ = f % NDC
                slot = f % 4
                W("scalar", "pe", w2_user[slot])
                DMA("scalar", "w2%d" % slot,
                    bass.AP(w2b, slot * 4096, [[4 * 4096, 128], [1, 4096]]),
                    bass.AP(w2bf_d, dc_ * 128 * 4096, [[4096, 128], [1, 4096]]))

            p3_load_x2(0)
            if p3_sc > 1:
                p3_load_x2(1)
            for f in range(min(4, p3_sc * NHC)):
                p3_load_w1(f)

            def gates_and_zg(src_T_done):
                nonlocal pG_free, pF_free
                ev3 = ACT(zg1s[:, :], pG[0:27, :], AF.Copy)
                pG_free = ("act", ev3)
                W("tensor", "act", ev3)
                WR("tensor", pF_free)
                tz = None
                for j in range(4):
                    tz = TRx(pF[:, j * 128: j * 128 + 27], zg1s[:, j * 128:(j + 1) * 128],
                             idnf[0:27, 0:27], inc="pe" if j == 3 else None)
                W("vector", "pe", tz)
                dzt = OP("vector", lambda e,
                         i=bass.AP(pF, 0, [[512, 128], [128, 4], [1, 27]]),
                         o=bass.AP(zts, 0, [[108, 128], [27, 4], [1, 27]]):
                         e.tensor_copy(o, i), inc="dve")
                pF_free = ("dve", dzt)
                last = dzt
                for sub in range(4):
                    W("vector", "dve", last)
                    dn = OP("vector", lambda e, s=sub:
                            e.reduce_max(statc(C_NGMX), zts[:, s * 27 + 24: s * 27 + 27], AX.X, negate=True), inc="dve")
                    W("scalar", "dve", dn)
                    ag = ACT(gex[:, sub * 3:(sub + 1) * 3], zts[:, sub * 27 + 24: sub * 27 + 27],
                             AF.Exp, bias=statc(C_NGMX))
                    W("vector", "act", ag)
                    ds = OP("vector", lambda e, s=sub:
                            e.reduce_sum(statc(C_GSUM), gex[:, s * 3:(s + 1) * 3], AX.X), inc="dve")
                    W("vector", "dve", ds)
                    dr = OP("vector", lambda e: e.reciprocal(statc(C_GSC), statc(C_GSUM)), inc="dve")
                    W("vector", "dve", dr)
                    dsc = OP("vector", lambda e: e.tensor_scalar(statc(C_GSC), statc(C_GSC), 0.125, None, op0=ALU.mult), inc="dve")
                    W("vector", "dve", dsc)
                    for ee in range(3):
                        last = OP("vector", lambda e, s=sub, x=ee:
                                  e.tensor_scalar(zgt[:, s * 24 + x * 8: s * 24 + x * 8 + 8],
                                                  zts[:, s * 27 + x * 8: s * 27 + x * 8 + 8],
                                                  gex[:, s * 3 + x: s * 3 + x + 1],
                                                  statc(C_GSC), op0=ALU.mult, op1=ALU.mult), inc="dve")
                return last

            def zg_transpose_out(dst, dzgt):
                nonlocal pT_free
                W("tensor", "dve", dzgt)
                WR("tensor", pT_free)
                tz2 = None
                for j in range(4):
                    tz2 = TRx(pT[0:24, j * 128:(j + 1) * 128], zgt[:, j * 24:(j + 1) * 24],
                              idnb[:, :], inc="pe" if j == 3 else None)
                W("vector", "pe", tz2)
                ev = OP("vector", lambda e,
                        i=bass.AP(pT, 0, [[1024, 24], [128, 4], [1, 128]]),
                        o=bass.AP(dst, 0, [[512, 24], [128, 4], [1, 128]]):
                        e.tensor_copy(o, i), inc="dve")
                pT_free = ("dve", ev)
                return ev

            p3_tr_last = [0]
            xl2T_ready = [0, 0]

            ln2_dxls = {}

            def emit_ln2A(sc, sub):
                slot = sc % 2
                if sub == 0:
                    WD("vector", "x2l%d" % slot)
                    WD("scalar", "x2l%d" % slot)
                    W("vector", "pe", p3_tr_last[0])
                    ln2_dxls[sc] = []
                xi = bass.AP(x2in, slot * 4096 + sub * 1024, [[2 * 4096, 128], [1, 1024]])
                dxl, asq = ln_chain("vector", "scalar", xi, xsq3,
                                    xlt3[:, sub * 1024:(sub + 1) * 1024])
                ln2_dxls[sc].append(dxl)
                if sub == 3:
                    x2_user_a[slot] = asq

            def emit_ln2B(sc):
                nonlocal pT_free
                slot = sc % 2
                dxls = ln2_dxls.pop(sc)
                for sub in range(4):
                    W("tensor", "dve", dxls[sub])
                    WR("tensor", pT_free)
                    tr = None
                    for dc in range(NDC):
                        tr = TRx(pT[:, dc * 128:(dc + 1) * 128],
                                 xlt3[:, sub * 1024 + dc * 128: sub * 1024 + dc * 128 + 128],
                                 idnb[:, :], inc="pe" if dc == 7 else None)
                    W("vector", "pe", tr)
                    if sub == 0:
                        W("vector", "pe", xl2T_user[slot])
                    ev = OP("vector", lambda e,
                            i=bass.AP(pT, 0, [[1024, 128], [128, NDC], [1, 128]]),
                            o=bass.AP(xl2T, slot * 4096 + sub * 128, [[2 * 4096, 128], [512, NDC], [1, 128]]):
                            e.tensor_copy(o, i), inc="dve")
                    pT_free = ("dve", ev)
                p3_tr_last[0] = tr
                xl2T_ready[slot] = ev

            for _sub in range(4):
                emit_ln2A(0, _sub)
            emit_ln2B(0)
            z1_res = {}

            def emit_z1(zc):
                zslot = zc % 2
                WD("tensor", "sm")
                W("tensor", "dve", xl2T_ready[zslot])
                WR("tensor", pG_free)
                mz = None
                for dc in range(NDC):
                    mz = MM(pG[0:27, :], a1r1b[:, dc * 27:(dc + 1) * 27],
                            bass.AP(xl2T, zslot * 4096 + dc * 512, [[2 * 4096, 128], [1, 512]]),
                            start=(dc == 0), stop=(dc == 7), inc="pe" if dc == 7 else None)
                W("scalar", "pe", mz)
                z1_res[zc] = gates_and_zg(mz)

            emit_z1(0)
            for sc in range(p3_sc):
                slot = sc % 2
                dzgt = z1_res.pop(sc)
                dzg1T = None

                if sc > 0:
                    W("scalar", "pe", hT_user)
                for hc in range(NHC):
                    flat = sc * NHC + hc
                    wslot = flat % 4
                    bs = 2 * (hc % 2)
                    WD("tensor", "w1%d" % wslot)
                    WR("tensor", half_free[bs])
                    for dc in range(NDC):
                        MM(HS[bs], w1b[:, wslot * 1024 + dc * 128: wslot * 1024 + dc * 128 + 128],
                           bass.AP(xl2T, slot * 4096 + dc * 512, [[2 * 4096, 128], [1, 512]]),
                           start=(dc == 0), stop=False, inc=None)
                    if hc == 0:
                        dzg1T = zg_transpose_out(zg1T, dzgt)
                        W("tensor", "dve", dzg1T)
                    mh = MM(HS[bs], b1ab[0:24, hc * 128:(hc + 1) * 128], zg1T[0:24, :],
                            start=False, stop=True, inc="pe")
                    w1_user[wslot] = mh
                    p3_load_w1(flat + 4)
                    W("scalar", "pe", mh)
                    rl = ACT(hT[:, hc * 512:(hc + 1) * 512], HS[bs], AF.Relu,
                             bias=b1cs[:, hc:hc + 1])
                    half_free[bs] = ("act", rl)
                    relu_done[hc % 2] = rl
                    if hc in (8, 11, 14, 17) and sc + 1 < p3_sc:
                        emit_ln2A(sc + 1, (hc - 8) // 3)
                    if hc == 15:
                        W("tensor", "act", rl)
                        WR("tensor", pG_free)
                        for hz in range(NHC // 2):
                            MM(pG[0:27, :], a2r2b[:, hz * 27:(hz + 1) * 27],
                               hT[:, hz * 512:(hz + 1) * 512],
                               start=(hz == 0), stop=False, inc=None)
                xl2T_user[slot] = CNT["pe"]
                relu_all = CNT["act"]

                if sc == 0:
                    p3_load_w2(0)
                    p3_load_w2(1)
                    p3_load_w2(2)
                    p3_load_w2(3)
                if sc + 1 < p3_sc:
                    emit_ln2B(sc + 1)

                W("tensor", "act", relu_all)
                mz2 = None
                for hc in range(NHC // 2, NHC):
                    mz2 = MM(pG[0:27, :], a2r2b[:, hc * 27:(hc + 1) * 27],
                             hT[:, hc * 512:(hc + 1) * 512],
                             start=False, stop=(hc == NHC - 1), inc="pe" if hc == NHC - 1 else None)
                W("scalar", "pe", mz2)
                dzgt2 = gates_and_zg(mz2)
                dzg2T = None

                pend_tr = []

                def emit_otr():
                    nonlocal pF_free
                    if not pend_tr:
                        return None
                    d_, dev_ = pend_tr.pop(0)
                    W("tensor", "dve", dev_)
                    WR("tensor", pF_free)
                    to = None
                    for j in range(4):
                        to = TRx(pF[:, j * 128:(j + 1) * 128],
                                 aob3[:, (d_ % 2) * 512 + j * 128: (d_ % 2) * 512 + j * 128 + 128],
                                 idnf[:, :], inc="pe" if j == 3 else None)
                    ao3_user[d_ % 2] = to
                    W("vector", "pe", to)
                    if d_ == 0:
                        WD("vector", "o")
                    adds3 = OP("vector", lambda e, d=d_, sl=slot:
                               e.tensor_add(bass.AP(outb, d * 128, [[4096, 128], [1024, 4], [1, 128]]),
                                            bass.AP(pF, 0, [[512, 128], [128, 4], [1, 128]]),
                                            bass.AP(x2in, sl * 4096 + d * 128, [[2 * 4096, 128], [1024, 4], [1, 128]])),
                               inc="dve")
                    pF_free = ("dve", adds3)
                    return adds3

                for dc in range(NDC):
                    w2slot = (sc * NDC + dc) % 4
                    bs = 2 * (dc % 2) + 1
                    WD("tensor", "w2%d" % w2slot)
                    WR("tensor", half_free[bs])
                    for hc in range(NHC):
                        MM(HS[bs], w2b[:, w2slot * 4096 + hc * 128: w2slot * 4096 + hc * 128 + 128],
                           hT[:, hc * 512:(hc + 1) * 512],
                           start=(hc == 0), stop=False, inc=None)
                    if dc == 0:
                        dzg2T = zg_transpose_out(zg2T, dzgt2)
                        W("tensor", "dve", dzg2T)
                    mo = MM(HS[bs], b2ab[0:24, dc * 128:(dc + 1) * 128], zg2T[0:24, :],
                            start=False, stop=True, inc="pe")
                    w2_user[w2slot] = mo
                    p3_load_w2(sc * NDC + dc + 4)
                    if dc == 4 and sc + 1 < p3_sc:
                        emit_z1(sc + 1)
                    W("vector", "pe", mo)
                    W("vector", "pe", ao3_user[dc % 2])
                    dev = OP("vector", lambda e, d=dc, b=HS[bs], sl=dc % 2:
                             e.tensor_scalar(aob3[:, sl * 512: sl * 512 + 512], b,
                                             b2cs[:, d:d + 1], None, op0=ALU.add), inc="dve")
                    half_free[bs] = ("dve", dev)
                    obank_free[dc % 2] = dev
                    emit_otr()
                    pend_tr.append((dc, dev))
                adds3 = emit_otr()
                hT_user = CNT["pe"]
                x2_user[slot] = ("dve", adds3)
                W("sync", "dve", adds3)
                DMA("sync", "o",
                    bass.AP(y_d, sc * 512 * D, [[D, 128], [128 * D, 4], [1, 1024]]),
                    bass.AP(outb, 0, [[4096, 128], [1024, 4], [1, 1024]]))
                if sc + 2 < p3_sc:
                    p3_load_x2(sc + 2)

            quiesce()
            flush_block()

    return nc


def _host_prep(ins):
    f32 = np.float32
    x = np.ascontiguousarray(np.asarray(ins["L0_attn_x"], dtype=f32))
    wqkv = np.asarray(ins["L0_attn_W_qkv"], dtype=f32)
    wo = np.asarray(ins["L0_attn_W_o"], dtype=f32)
    w1 = np.asarray(ins["L0_ffn_W1"], dtype=f32)
    b1 = np.asarray(ins["L0_ffn_b1"], dtype=f32)
    a1 = np.asarray(ins["L0_ffn_A1"], dtype=f32)
    B1 = np.asarray(ins["L0_ffn_B1"], dtype=f32)
    r1 = np.asarray(ins["L0_ffn_R1"], dtype=f32)
    w2 = np.asarray(ins["L0_ffn_W2"], dtype=f32)
    b2 = np.asarray(ins["L0_ffn_b2"], dtype=f32)
    a2 = np.asarray(ins["L0_ffn_A2"], dtype=f32)
    B2 = np.asarray(ins["L0_ffn_B2"], dtype=f32)
    r2 = np.asarray(ins["L0_ffn_R2"], dtype=f32)

    d = {}
    d["wqkvT"] = np.ascontiguousarray(wqkv.T)
    d["woT"] = np.ascontiguousarray(wo.T)
    d["w1r"] = np.ascontiguousarray(
        w1.reshape(8, 128, 32, 128).transpose(2, 1, 0, 3)).reshape(32, 128, 1024)
    d["w2r"] = np.ascontiguousarray(
        w2.reshape(32, 128, 8, 128).transpose(2, 1, 0, 3)).reshape(8, 128, 4096)
    a1all = a1.transpose(1, 0, 2).reshape(1024, 24)
    a2all = a2.transpose(1, 0, 2).reshape(4096, 24)
    a1r1 = np.zeros((128, 8 * 27), f32)
    for dc in range(8):
        a1r1[:, dc * 27:dc * 27 + 24] = a1all[dc * 128:(dc + 1) * 128, :]
        a1r1[:, dc * 27 + 24:dc * 27 + 27] = r1[dc * 128:(dc + 1) * 128, :]
    a2r2 = np.zeros((128, 32 * 27), f32)
    for hc in range(32):
        a2r2[:, hc * 27:hc * 27 + 24] = a2all[hc * 128:(hc + 1) * 128, :]
        a2r2[:, hc * 27 + 24:hc * 27 + 27] = r2[hc * 128:(hc + 1) * 128, :]
    d["a1r1"] = a1r1
    d["a2r2"] = a2r2
    d["b1a"] = np.ascontiguousarray(B1.reshape(24, 4096))
    d["b2a"] = np.ascontiguousarray(B2.reshape(24, 1024))
    d["b1c"] = np.ascontiguousarray(b1.reshape(32, 128).T)
    d["b2c"] = np.ascontiguousarray(b2.reshape(8, 128).T)
    d["idnb"] = np.eye(128, dtype=ml_dtypes.bfloat16)
    d["idnf"] = np.eye(128, dtype=f32)
    o2 = np.zeros((128, 2), f32)
    o2[0:64, 0] = 1.0
    o2[64:128, 1] = 1.0
    d["ones2"] = o2.astype(ml_dtypes.bfloat16)
    d["ones64r"] = np.ones((1, 64), f32)
    d["ones128c"] = np.ones((128, 1), ml_dtypes.bfloat16)
    d["ones128r"] = np.ones((1, 128), f32)
    kr = np.arange(128)[:, None]
    qc = np.arange(512)[None, :]
    m = np.concatenate([(qc >= kr + 128 * j).astype(f32) for j in range(4)], axis=1)
    d["masks"] = m.astype(ml_dtypes.bfloat16)
    return x, d


def kernel(**ins):
    x, shared = _host_prep(ins)
    nc = build_nc()
    in_maps = []
    for b in range(8):
        m = dict(shared)
        m["x"] = np.ascontiguousarray(x[b])
        in_maps.append(m)
    res = run_bass_kernel_spmd(nc, in_maps, core_ids=list(range(8)))
    return np.stack([np.asarray(res.results[b]["y"]) for b in range(8)], axis=0)


# version 47 (final)
# speedup vs baseline: 1.2042x; 1.2042x over previous
import contextlib
import numpy as np
import ml_dtypes
import concourse.bass as bass
import concourse.mybir as mybir
from concourse.bass_utils import run_bass_kernel_spmd

F32 = mybir.dt.float32
BF16 = mybir.dt.bfloat16
AF = mybir.ActivationFunctionType
ALU = mybir.AluOpType
AX = mybir.AxisListType

D = 1024
S = 4096
FH = 4096
NDC = 8
NHC = 32
VW = 1040
EPS = 1e-5
LNC = float(np.log(0.125 * 1.04))


def build_nc(p1_sc=8, p2_g=8, p2_h=16, p3_sc=8, debug=False):
    nc = bass.Bass("TRN2", target_bir_lowering=False)

    x_d = nc.dram_tensor("x", [S, D], F32, kind="ExternalInput")
    wqkvT_d = nc.dram_tensor("wqkvT", [D, 3 * D], F32, kind="ExternalInput")
    woT_d = nc.dram_tensor("woT", [D, D], F32, kind="ExternalInput")
    w1r_d = nc.dram_tensor("w1r", [NHC, 128, NDC * 128], F32, kind="ExternalInput")
    w2r_d = nc.dram_tensor("w2r", [NDC, 128, NHC * 128], F32, kind="ExternalInput")
    a1r1_d = nc.dram_tensor("a1r1", [128, NDC * 27], F32, kind="ExternalInput")
    a2r2_d = nc.dram_tensor("a2r2", [128, NHC * 27], F32, kind="ExternalInput")
    b1a_d = nc.dram_tensor("b1a", [24, FH], F32, kind="ExternalInput")
    b2a_d = nc.dram_tensor("b2a", [24, D], F32, kind="ExternalInput")
    b1c_d = nc.dram_tensor("b1c", [128, NHC], F32, kind="ExternalInput")
    b2c_d = nc.dram_tensor("b2c", [128, NDC], F32, kind="ExternalInput")
    idnb_d = nc.dram_tensor("idnb", [128, 128], BF16, kind="ExternalInput")
    idnf_d = nc.dram_tensor("idnf", [128, 128], F32, kind="ExternalInput")
    ones2_d = nc.dram_tensor("ones2", [128, 2], BF16, kind="ExternalInput")
    ones64r_d = nc.dram_tensor("ones64r", [1, 64], F32, kind="ExternalInput")
    ones128r_d = nc.dram_tensor("ones128r", [1, 128], F32, kind="ExternalInput")
    ones128c_d = nc.dram_tensor("ones128c", [128, 1], BF16, kind="ExternalInput")
    masks_d = nc.dram_tensor("masks", [128, 2048], BF16, kind="ExternalInput")

    y_d = nc.dram_tensor("y", [S, D], F32, kind="ExternalOutput")
    sk = "ExternalOutput" if debug else "Internal"
    xlnT_d = nc.dram_tensor("xlnT", [D, S], BF16, kind=sk)
    kT_d = nc.dram_tensor("kT", [D, S], BF16, kind=sk)
    x2_d = nc.dram_tensor("x2", [S, D], F32, kind=sk)
    w1bf_d = nc.dram_tensor("w1bf", [NHC, 128, NDC * 128], BF16, kind="Internal")
    w2bf_d = nc.dram_tensor("w2bf", [NDC, 128, NHC * 128], BF16, kind="Internal")

    EM = {k: [] for k in ("sync", "gpsimd", "tensor", "scalar", "vector")}
    SYNC_SEMS = ["c", "x0", "x1", "xlt0", "xlt1", "kv0", "kv1", "kv2",
                 "xg0", "xg1", "kh0", "kh1", "kh2", "kh3", "xi", "x2s",
                 "x2l0", "x2l1", "smc", "o"]
    GP_SEMS = ["wq", "wkv", "wo", "sm", "wc", "wcs", "w10", "w11", "w12", "w13",
               "w20", "w21", "w22", "w23"]
    CNT = {s: 0 for s in ["pe", "act", "dve"] + SYNC_SEMS + GP_SEMS}
    SEMH = {}

    def q(qn, fn):
        EM[qn].append(fn)

    WHW = {k: {} for k in EM}

    def W(qn, sem, val):
        if val is None or val <= 0:
            return
        if WHW[qn].get(sem, 0) >= val:
            return
        WHW[qn][sem] = val
        q(qn, lambda e, s=sem, v=val: e.wait_ge(SEMH[s], v))

    def WR(qn, res):
        W(qn, res[0], res[1])

    def WD(qn, sem):
        W(qn, sem, CNT[sem])

    def quiesce():
        for s in SYNC_SEMS:
            WD("sync", s)
        for s in GP_SEMS:
            WD("gpsimd", s)
            WD("sync", s)

    def OP(qn, build, inc=None, by=1):
        if inc is None:
            q(qn, lambda e, b=build: b(e))
            return None
        CNT[inc] += by
        q(qn, lambda e, b=build, s=inc, n=by: b(e).then_inc(SEMH[s], n))
        return CNT[inc]

    def DMA(qn, sem, out_ap, in_ap):
        return OP(qn, lambda e, o=out_ap, i=in_ap: e.dma_start(o, i), inc=sem, by=16)

    def MM(out_ap, lhsT, rhs, start, stop, inc=None):
        return OP("tensor",
                  lambda e, o=out_ap, l=lhsT, r=rhs, st=start, sp=stop:
                  e.matmul(o, l, r, start=st, stop=sp), inc=inc)

    def TRx(out_ap, in_ap, idn, inc=None):
        return OP("tensor",
                  lambda e, o=out_ap, i=in_ap, d=idn: e.transpose(o, i, d), inc=inc)

    def ACT(out_ap, in_ap, func, bias=0.0, scale=1.0):
        return OP("scalar",
                  lambda e, o=out_ap, i=in_ap, f=func, b=bias, s=scale:
                  e.activation(o, i, f, bias=b, scale=s), inc="act")

    def flush_block():
        with nc.Block() as blk:
            for name, dec in (("sync", blk.sync), ("gpsimd", blk.gpsimd),
                              ("tensor", blk.tensor), ("scalar", blk.scalar),
                              ("vector", blk.vector)):
                fns = EM[name]
                EM[name] = []
                if fns:
                    def body(eng, fns=fns):
                        for f in fns:
                            f(eng)
                    dec(body)

    NONE = ("pe", 0)

    with contextlib.ExitStack() as ES:
        for s in CNT:
            SEMH[s] = ES.enter_context(nc.semaphore(s))
        pAB = ES.enter_context(nc.psum_tensor("pAB", [128, 1024], F32))
        pCD = ES.enter_context(nc.psum_tensor("pCD", [128, 1024], F32))
        pE = ES.enter_context(nc.psum_tensor("pE", [128, 512], F32))
        pF = ES.enter_context(nc.psum_tensor("pF", [128, 512], F32))
        pG = ES.enter_context(nc.psum_tensor("pG", [128, 512], F32))
        pT = ES.enter_context(nc.psum_tensor("pT", [128, 1024], BF16))
        HS = [pAB[:, 0:512], pAB[:, 512:1024], pCD[:, 0:512], pCD[:, 512:1024]]
        half_free = [NONE, NONE, NONE, NONE]
        pE_free = NONE
        pF_free = NONE
        pG_free = NONE
        pT_free = NONE

        idnb = ES.enter_context(nc.sbuf_tensor("idnb_s", [128, 128], BF16))
        idnf = ES.enter_context(nc.sbuf_tensor("idnf_s", [128, 128], F32))
        ones2 = ES.enter_context(nc.sbuf_tensor("ones2_s", [128, 2], BF16))
        ones64r = ES.enter_context(nc.sbuf_tensor("ones64r_s", [1, 64], F32))
        ones128r = ES.enter_context(nc.sbuf_tensor("ones128r_s", [1, 128], F32))
        ones128c = ES.enter_context(nc.sbuf_tensor("ones128c_s", [128, 1], BF16))
        stat = ES.enter_context(nc.sbuf_tensor("stat", [128, 32], F32))
        srow = ES.enter_context(nc.sbuf_tensor("srow", [1, 16], F32))
        rowb = ES.enter_context(nc.sbuf_tensor("rowb", [1, 1024], F32))
        wqTb = ES.enter_context(nc.sbuf_tensor("wqTb", [128, NDC * 1024], BF16))
        knmc = ES.enter_context(nc.sbuf_tensor("knmc", [2, 64], F32))

        C_S1, C_S2, C_NM, C_VE, C_M2, C_VAR, C_RSTD, C_NMR, C_NEGC, C_NGMX, C_GSUM, C_GSC = range(12)
        KMX_COL = 16
        QMX_COL = 17
        R_MK2, R_LNMK, R_MQ2, R_LNQ, R_LSUM, R_CG, R_NEGC1 = range(7)

        def statc(i):
            return stat[:, i:i + 1]

        def srowc(i):
            return srow[0:1, i:i + 1]

        xsq_user = [0]

        def ln_chain(qv, qa, xi, xsq_t, out_ap):
            OP(qv, lambda e, i=xi: e.reduce_sum(statc(C_S1), i, AX.X), inc="dve")
            W(qa, "dve", xsq_user[0])
            a0 = ACT(xsq_t[:, :], xi, AF.Square)
            W(qv, "act", a0)
            d1 = OP(qv, lambda e, t=xsq_t: e.reduce_sum(statc(C_S2), t[:, :], AX.X), inc="dve")
            xsq_user[0] = d1
            W(qv, "dve", d1)
            OP(qv, lambda e: e.tensor_scalar(statc(C_NM), statc(C_S1), -1.0 / D, None, op0=ALU.mult), inc="dve")
            d1b = OP(qv, lambda e: e.tensor_scalar(statc(C_VE), statc(C_S2), 1.0 / D, EPS, op0=ALU.mult, op1=ALU.add), inc="dve")
            W(qv, "dve", d1b)
            d2 = OP(qv, lambda e: e.tensor_mul(statc(C_M2), statc(C_NM), statc(C_NM)), inc="dve")
            W(qv, "dve", d2)
            d3 = OP(qv, lambda e: e.tensor_sub(statc(C_VAR), statc(C_VE), statc(C_M2)), inc="dve")
            W(qa, "dve", d3)
            a1 = ACT(statc(C_VAR), statc(C_VAR), AF.Ln)
            W(qa, "act", a1)
            a2 = ACT(statc(C_RSTD), statc(C_VAR), AF.Exp, scale=-0.5)
            W(qv, "act", a2)
            d4 = OP(qv, lambda e: e.tensor_mul(statc(C_NMR), statc(C_NM), statc(C_RSTD)), inc="dve")
            W(qv, "dve", d4)
            dout = OP(qv, lambda e, i=xi, o=out_ap:
                      e.tensor_scalar(o, i, statc(C_RSTD), statc(C_NMR), op0=ALU.mult, op1=ALU.add), inc="dve")
            return dout, a0

        S12 = contextlib.ExitStack()
        V = S12.enter_context(nc.sbuf_tensor("V", [128, 32 * VW], BF16))
        with contextlib.ExitStack() as S1:
            wkvTb = S1.enter_context(nc.sbuf_tensor("wkvTb", [128, NDC * 2048], BF16))
            xin1 = S1.enter_context(nc.sbuf_tensor("xin1", [128, 2 * 4096], F32))
            xsq = S1.enter_context(nc.sbuf_tensor("xsq", [128, 1024], F32))
            xlt = S1.enter_context(nc.sbuf_tensor("xlt", [128, 4096], BF16))
            xlT = S1.enter_context(nc.sbuf_tensor("xlT", [128, 2 * 4096], BF16))
            kvb = S1.enter_context(nc.sbuf_tensor("kvb", [128, 7 * 512], BF16))
            wstg = S1.enter_context(nc.sbuf_tensor("wstg", [128, 2 * 1024], BF16))
            ksq = S1.enter_context(nc.sbuf_tensor("ksq", [128, 512], BF16))

            DMA("sync", "c", idnb.ap(), idnb_d.ap())
            DMA("sync", "c", idnf.ap(), idnf_d.ap())
            DMA("sync", "c", ones2.ap(), ones2_d.ap())
            DMA("sync", "c", ones64r.ap(), ones64r_d.ap())
            DMA("sync", "c", ones128r.ap(), ones128r_d.ap())
            DMA("sync", "c", ones128c.ap(), ones128c_d.ap())
            DMA("gpsimd", "wq",
                bass.AP(wqTb, 0, [[NDC * 1024, 128], [1024, NDC], [1, 1024]]),
                bass.AP(wqkvT_d, 0, [[3 * D, 128], [128 * 3 * D, NDC], [1, 1024]]))
            DMA("gpsimd", "wkv",
                bass.AP(wkvTb, 0, [[NDC * 2048, 128], [2048, NDC], [1, 2048]]),
                bass.AP(wqkvT_d, 1024, [[3 * D, 128], [128 * 3 * D, NDC], [1, 2048]]))
            for i in range(64):
                wsrc, wdst = (w1r_d, w1bf_d) if i < 32 else (w2r_d, w2bf_d)
                off = (i % 32) * 128 * 1024
                slot = i % 2
                if i >= 2:
                    W("gpsimd", "wcs", 16 * (i - 1))
                ld = DMA("gpsimd", "wc",
                         bass.AP(wstg, slot * 1024, [[2 * 1024, 128], [1, 1024]]),
                         bass.AP(wsrc, off, [[1024, 128], [1, 1024]]))
                W("gpsimd", "wc", ld)
                DMA("gpsimd", "wcs",
                    bass.AP(wdst, off, [[1024, 128], [1, 1024]]),
                    bass.AP(wstg, slot * 1024, [[2 * 1024, 128], [1, 1024]]))
            for i in range(32):
                OP("vector", lambda e, a=bass.AP(V, i * VW + 64, [[32 * VW, 128], [65, 16], [1, 1]]):
                   e.memset(a, 1.0), inc="dve")

            WD("tensor", "c")

            xin_user = [NONE, NONE]
            xin_user_a = [0, 0]
            kvb_sq = [0, 0, 0, 0]
            xlT_user = [0, 0]
            kvb_user = [NONE] * 7
            ksq_user = 0

            def p1_load_x(sc):
                slot = sc % 2
                WR("sync", xin_user[slot])
                W("sync", "act", xin_user_a[slot])
                DMA("sync", "x%d" % slot,
                    bass.AP(xin1, slot * 4096, [[2 * 4096, 128], [1024, 4], [1, 1024]]),
                    bass.AP(x_d, sc * 512 * D, [[D, 128], [128 * D, 4], [1, 1024]]))

            p1_load_x(0)
            if p1_sc > 1:
                p1_load_x(1)

            def p1_transposes(sc, dxls):
                nonlocal pT_free
                slot = sc % 2
                for sub in range(4):
                    W("tensor", "dve", dxls[sub])
                    WR("tensor", pT_free)
                    tr = None
                    for dc in range(NDC):
                        tr = TRx(pT[:, dc * 128:(dc + 1) * 128],
                                 xlt[:, sub * 1024 + dc * 128: sub * 1024 + dc * 128 + 128],
                                 idnb[:, :], inc="pe" if dc == 7 else None)
                    W("vector", "pe", tr)
                    if sub == 0:
                        WD("vector", "xlt%d" % slot)
                        W("vector", "pe", xlT_user[slot])
                    ev = OP("vector", lambda e,
                            i=bass.AP(pT, 0, [[1024, 128], [128, NDC], [1, 128]]),
                            o=bass.AP(xlT, slot * 4096 + sub * 128, [[2 * 4096, 128], [512, NDC], [1, 128]]):
                            e.tensor_copy(o, i), inc="dve")
                    pT_free = ("dve", ev)
                dve_xlT = ev
                W("sync", "dve", dve_xlT)
                DMA("sync", "xlt%d" % slot,
                    bass.AP(xlnT_d, sc * 512, [[S, 128], [128 * S, NDC], [1, 512]]),
                    bass.AP(xlT, slot * 4096, [[2 * 4096, 128], [512, NDC], [1, 512]]))
                return dve_xlT

            dxls_pend = {}

            def p1_ln_sub(sc, sub):
                slot = sc % 2
                if sub == 0:
                    WD("vector", "x%d" % slot)
                    WD("scalar", "x%d" % slot)
                    dxls_pend[sc] = []
                xi = bass.AP(xin1, slot * 4096 + sub * 1024, [[2 * 4096, 128], [1, 1024]])
                dxl, asq = ln_chain("vector", "scalar", xi, xsq,
                                    xlt[:, sub * 1024:(sub + 1) * 1024])
                dxls_pend[sc].append(dxl)
                if sub == 3:
                    xin_user[slot] = ("dve", dxl)
                    xin_user_a[slot] = asq

            def p1_kvproj(sc, dve_xlT, ln_next):
                nonlocal ksq_user, pE_free, pT_free
                slot = sc % 2
                WD("tensor", "wkv")
                for fc in range(16):
                    bs = fc % 4
                    WR("tensor", half_free[bs])
                    if fc == 0:
                        W("tensor", "dve", dve_xlT)
                    mmv = None
                    for dc in range(NDC):
                        mmv = MM(HS[bs], wkvTb[:, dc * 2048 + fc * 128: dc * 2048 + fc * 128 + 128],
                                 bass.AP(xlT, slot * 4096 + dc * 512, [[2 * 4096, 128], [1, 512]]),
                                 start=(dc == 0), stop=(dc == 7), inc="pe" if dc == 7 else None)
                    kslot = fc % 4 if fc < 8 else 4 + fc % 3
                    W("scalar", "pe", mmv)
                    WD("scalar", "kv%d" % (kslot % 3))
                    WR("scalar", kvb_user[kslot])
                    if fc < 8:
                        W("scalar", "dve", kvb_sq[kslot])
                    evk = ACT(kvb[:, kslot * 512:(kslot + 1) * 512], HS[bs], AF.Copy)
                    half_free[bs] = ("act", evk)
                    if fc < 8:
                        W("scalar", "act", evk)
                        DMA("scalar", "kv%d" % (kslot % 3),
                            bass.AP(kT_d, fc * 128 * S + sc * 512, [[S, 128], [1, 512]]),
                            kvb[:, kslot * 512:(kslot + 1) * 512])
                        W("vector", "act", evk)
                        W("vector", "pe", ksq_user)
                        aq = OP("vector", lambda e, s=kvb[:, kslot * 512:(kslot + 1) * 512]:
                                e.tensor_mul(ksq[:, :], s, s), inc="dve")
                        kvb_sq[kslot] = aq
                        W("tensor", "dve", aq)
                        WR("tensor", pE_free)
                        nm = MM(pE[0:2, :], ones2[:, :], ksq[:, :], start=True, stop=True, inc="pe")
                        ksq_user = nm
                        W("vector", "pe", nm)
                        col = sc * 8 + fc
                        dmx = OP("vector", lambda e, c=col:
                                 e.reduce_max(knmc[0:2, c:c + 1], pE[0:2, :], AX.X), inc="dve")
                        pE_free = ("dve", dmx)
                        kvb_user[kslot] = ("kv%d" % (kslot % 3), CNT["kv%d" % (kslot % 3)])
                    else:
                        vfc = fc - 8
                        he = 2 * vfc
                        W("tensor", "act", evk)
                        WR("tensor", pT_free)
                        tr = None
                        for sub in range(4):
                            tr = TRx(pT[:, sub * 128:(sub + 1) * 128],
                                     kvb[:, kslot * 512 + sub * 128: kslot * 512 + sub * 128 + 128],
                                     idnb[:, :], inc="pe" if sub == 3 else None)
                        kvb_user[kslot] = ("pe", tr)
                        W("scalar", "pe", tr)
                        OP("scalar", lambda e,
                           i=bass.AP(pT, 0, [[1024, 128], [128, 4], [1, 64]]),
                           o=bass.AP(V, (sc * 4) * VW + he * 65, [[32 * VW, 128], [VW, 4], [1, 64]]):
                           e.activation(o, i, AF.Copy), inc="act")
                        ev2 = OP("scalar", lambda e,
                                 i=bass.AP(pT, 64, [[1024, 128], [128, 4], [1, 64]]),
                                 o=bass.AP(V, (sc * 4) * VW + (he + 1) * 65, [[32 * VW, 128], [VW, 4], [1, 64]]):
                                 e.activation(o, i, AF.Copy), inc="act")
                        pT_free = ("act", ev2)
                    if ln_next is not None and fc in (9, 11, 13, 15):
                        p1_ln_sub(ln_next, (fc - 9) // 2)
                xlT_user[slot] = CNT["pe"]

            for sub in range(4):
                p1_ln_sub(0, sub)
            for sc in range(p1_sc):
                if sc + 2 < p1_sc:
                    p1_load_x(sc + 2)
                dve_xlT = p1_transposes(sc, dxls_pend.pop(sc))
                p1_kvproj(sc, dve_xlT, sc + 1 if sc + 1 < p1_sc else None)

            W("vector", "dve", CNT["dve"])
            d5 = OP("vector", lambda e, n=8 * p1_sc:
                    e.reduce_max(stat[0:2, KMX_COL:KMX_COL + 1], knmc[0:2, 0:n], AX.X), inc="dve")
            W("tensor", "dve", d5)
            WR("tensor", pE_free)
            t5 = TRx(pE[0:1, 0:2], stat[0:2, KMX_COL:KMX_COL + 1], idnf[0:2, 0:2], inc="pe")
            W("vector", "pe", t5)
            d6 = OP("vector", lambda e: e.reduce_max(srowc(R_MK2), pE[0:1, 0:2], AX.X), inc="dve")
            pE_free = ("dve", d6)
            W("scalar", "dve", d6)
            ACT(srowc(R_LNMK), srowc(R_MK2), AF.Ln)

            quiesce()
            flush_block()

        with contextlib.ExitStack() as S2:
            woTb = S2.enter_context(nc.sbuf_tensor("woTb", [128, NDC * 1024], BF16))
            masks2 = S2.enter_context(nc.sbuf_tensor("masks2", [128, 2048], BF16))
            xlnTg = S2.enter_context(nc.sbuf_tensor("xlnTg", [128, 2 * 4096], BF16))
            qTg = S2.enter_context(nc.sbuf_tensor("qTg", [128, 8 * 512], BF16))
            qsq = S2.enter_context(nc.sbuf_tensor("qsq", [128, 512], BF16))
            Kh = S2.enter_context(nc.sbuf_tensor("Kh", [128, 4096], BF16))
            probsT = S2.enter_context(nc.sbuf_tensor("probsT", [128, 4 * 1024], BF16))
            headsT = S2.enter_context(nc.sbuf_tensor("headsT", [128, 2 * 8 * 512], BF16))
            rb = S2.enter_context(nc.sbuf_tensor("rb", [128, 512], F32))
            rb2 = S2.enter_context(nc.sbuf_tensor("rb2", [128, 512], F32))
            aob = S2.enter_context(nc.sbuf_tensor("aob", [128, 2 * 512], F32))
            xing = S2.enter_context(nc.sbuf_tensor("xing", [128, 4096], F32))
            x2b = S2.enter_context(nc.sbuf_tensor("x2b", [128, 4096], F32))

            DMA("gpsimd", "wo",
                bass.AP(woTb, 0, [[NDC * 1024, 128], [1024, NDC], [1, 1024]]),
                bass.AP(woT_d, 0, [[D, 128], [128 * D, NDC], [1, 1024]]))
            DMA("sync", "c", masks2.ap(), masks_d.ap())
            WD("vector", "c")

            xlnTg_user = [0, 0]
            Kh_user = [0, 0, 0, 0]
            headsT_user = [0, 0]
            pend_wo = []
            xing_user = 0
            qTg_user = 0
            qsq_user = [0, 0]
            norm_done = [0, 0]
            probs_user = [0, 0, 0, 0]
            sq2 = [0]
            rowb_user = 0
            rb_user = 0
            ao_user = [0, 0]

            def p2_load_xlnTg(g):
                slot = g % 2
                W("sync", "pe", xlnTg_user[slot])
                DMA("sync", "xg%d" % slot,
                    bass.AP(xlnTg, slot * 4096, [[2 * 4096, 128], [512, NDC], [1, 512]]),
                    bass.AP(xlnT_d, g * 512, [[S, 128], [128 * S, NDC], [1, 512]]))

            def p2_load_Kh(g, h):
                kmax = 512 * (g + 1)
                if g < 4:
                    slot = h % 4
                    off = (h % 2) * 64 * 4096 + ((h // 2) % 2) * 2048
                    W("sync", "pe", Kh_user[slot])
                    DMA("sync", "kh%d" % slot,
                        bass.AP(Kh, off, [[4096, 64], [1, kmax]]),
                        bass.AP(kT_d, h * 64 * S, [[S, 64], [1, kmax]]))
                else:
                    slot = h % 2
                    W("sync", "pe", max(Kh_user[slot], Kh_user[slot + 2]))
                    DMA("sync", "kh%d" % slot,
                        bass.AP(Kh, slot * 64 * 4096, [[4096, 64], [1, kmax]]),
                        bass.AP(kT_d, h * 64 * S, [[S, 64], [1, kmax]]))

            def emit_qproj(g):
                nonlocal pG_free, qn_last
                gslot = g % 2
                WD("tensor", "wq")
                WD("tensor", "xg%d" % gslot)
                pend_q = []

                def emit_qn(is_last):
                    nonlocal qn_last
                    if not pend_q:
                        return
                    m2_, first_ = pend_q.pop()
                    W("tensor", "dve", m2_)
                    if first_:
                        WR("tensor", pG_free)
                    qn_last = MM(pG[0:1, :], ones128c[:, 0:1], qsq[:, :],
                                 start=first_, stop=is_last, inc="pe")
                    qsq_user[0] = qn_last

                for fc in range(8):
                    bs = fc % 4
                    WR("tensor", half_free[bs])
                    mmq = None
                    for dc in range(NDC):
                        mmq = MM(HS[bs], wqTb[:, dc * 1024 + fc * 128: dc * 1024 + fc * 128 + 128],
                                 bass.AP(xlnTg, gslot * 4096 + dc * 512, [[2 * 4096, 128], [1, 512]]),
                                 start=(dc == 0), stop=(dc == 7), inc="pe" if dc == 7 else None)
                    W("vector", "pe", mmq)
                    if fc == 0:
                        W("vector", "pe", qTg_user)
                    q1 = qTg[:, fc * 512:fc * 512 + 512]
                    c2 = OP("vector", lambda e, o=q1, b=HS[bs]: e.tensor_copy(o, b), inc="dve")
                    half_free[bs] = ("dve", c2)
                    emit_qn(False)
                    W("vector", "dve", c2)
                    W("vector", "pe", qsq_user[0])
                    m2 = OP("vector", lambda e, i=q1: e.tensor_mul(qsq[:, :], i, i), inc="dve")
                    pend_q.append((m2, fc == 0))
                emit_qn(True)
                xlnTg_user[gslot] = CNT["pe"]
                if g + 2 < p2_g:
                    p2_load_xlnTg(g + 2)

                W("vector", "pe", qn_last)
                d8 = OP("vector", lambda e: e.reduce_max(srowc(R_MQ2), pG[0:1, 0:512], AX.X), inc="dve")
                W("scalar", "dve", d8)
                a3 = ACT(srowc(R_LNQ), srowc(R_MQ2), AF.Ln)
                W("vector", "act", a3)
                d9 = OP("vector", lambda e: e.tensor_add(srowc(R_LSUM), srowc(R_LNQ), srowc(R_LNMK)), inc="dve")
                W("scalar", "dve", d9)
                a4 = ACT(srowc(R_CG), srowc(R_LSUM), AF.Exp, scale=0.5, bias=np.float32(LNC))
                W("vector", "act", a4)
                d10 = OP("vector", lambda e: e.tensor_scalar(srowc(R_NEGC1), srowc(R_CG), -1.0, None, op0=ALU.mult), inc="dve")
                W("tensor", "dve", d10)
                t9 = MM(pG[0:128, 3:4], ones128r[0:1, :], srowc(R_NEGC1), start=True, stop=True, inc="pe")
                W("scalar", "pe", t9)
                a5 = ACT(statc(C_NEGC), pG[0:128, 3:4], AF.Copy)
                pG_free = ("act", a5)
                W("scalar", "act", a5)

            qn_last = 0
            p2_load_xlnTg(0)
            if p2_g > 1:
                p2_load_xlnTg(1)
            emit_qproj(0)
            for h0 in range(4):
                p2_load_Kh(0, h0)

            for g in range(p2_g):
                if g == 0:
                    DMA("sync", "xi",
                        bass.AP(xing, 0, [[4096, 128], [1024, 4], [1, 1024]]),
                        bass.AP(x_d, 0, [[D, 128], [128 * D, 4], [1, 1024]]))
                pairs = 2 * g + 2
                pending_norm = []

                def emit_norm():
                    nonlocal pG_free, rowb_user, rb_user
                    if not pending_norm:
                        return
                    ha, pva, pvla = pending_norm.pop(0)
                    hb, pvb2, pvlb = pending_norm.pop(0)
                    W("vector", "pe", max(pvla, pvlb))
                    W("vector", "pe", rowb_user)
                    OP("vector", lambda e: e.tensor_copy(rowb[0:1, 0:512], pva[64:65, :]), inc="dve")
                    cbs = OP("vector", lambda e: e.tensor_copy(rowb[0:1, 512:1024], pvb2[64:65, :]), inc="dve")
                    W("tensor", "dve", cbs)
                    WR("tensor", pG_free)
                    MM(pG[0:64, :], ones64r[0:1, :], rowb[0:1, 0:512],
                       start=True, stop=True)
                    t10 = MM(pG[64:128, :], ones64r[0:1, :], rowb[0:1, 512:1024],
                             start=True, stop=True, inc="pe")
                    rowb_user = t10
                    W("vector", "pe", t10)
                    W("vector", "dve", rb_user)
                    cb = OP("vector", lambda e: e.tensor_copy(rb[:, :], pG[:, :]), inc="dve")
                    pG_free = ("dve", cb)
                    W("vector", "dve", cb)
                    rcp = OP("vector", lambda e: e.reciprocal(rb2[:, :], rb[:, :]), inc="dve")
                    W("vector", "dve", rcp)
                    W("vector", "pe", headsT_user[g % 2])
                    hoff = (g % 2) * 4096
                    mul1 = OP("vector", lambda e, h2=ha, pv=pva, ho=hoff:
                              e.tensor_mul(headsT[0:64, ho + (h2 // 2) * 512: ho + (h2 // 2) * 512 + 512],
                                           pv[0:64, :], rb2[0:64, :]), inc="dve")
                    rb_user = OP("vector", lambda e, h2=hb, pv=pvb2, ho=hoff:
                                 e.tensor_mul(headsT[64:128, ho + (h2 // 2) * 512: ho + (h2 // 2) * 512 + 512],
                                              pv[0:64, :], rb2[64:128, :]), inc="dve")
                    norm_done[0] = mul1
                    norm_done[1] = rb_user

                for hp in range(p2_h // 2):
                    h1, h2 = 2 * hp, 2 * hp + 1
                    ks0 = (2 * hp) % 4 if g < 4 else 0
                    chalf = (hp % 2) * 2048 if g < 4 else 0
                    WD("tensor", "kh%d" % ks0)
                    WD("tensor", "kh%d" % (ks0 + 1))
                    pend = []
                    pv_done = {}

                    def emit_pv():
                        if not pend:
                            return
                        hh, pvb_, ps_, pi_, gate0_, gate1_ = pend.pop(0)
                        if pi_ == 0:
                            W("tensor", "dve", norm_done[hh % 2])
                        mpv_ = None
                        for half in range(2):
                            WR("tensor", gate0_ if half == 0 else gate1_)
                            kc = 2 * pi_ + half
                            q0 = max(0, (kc - 4 * g)) * 128
                            if pi_ == 0 and half == 0:
                                q0 = 0
                            mpv_ = MM(pvb_[0:65, q0:],
                                      V[:, kc * VW + hh * 65: kc * VW + hh * 65 + 65],
                                      probsT[:, ps_ * 1024 + half * 512 + q0: ps_ * 1024 + half * 512 + 512],
                                      start=(pi_ == 0 and half == 0),
                                      stop=(pi_ == pairs - 1 and half == 1),
                                      inc="pe" if half == 1 else None)
                        pv_done[(hh, pi_)] = mpv_
                        probs_user[ps_] = mpv_

                    for pi in range(pairs):
                        for sub, (hh, pvb_) in enumerate(((h1, pE), (h2, pF))):
                            b3 = sq2[0] % 2
                            ps = sq2[0] % 4
                            sq2[0] += 1
                            pbank = pAB if b3 == 0 else pCD
                            mms = []
                            for half in range(2):
                                WR("tensor", half_free[2 * b3 + half])
                                kc = 2 * pi + half
                                q0 = max(0, (kc - 4 * g)) * 128
                                mms.append(MM(pbank[:, half * 512 + q0:half * 512 + 512],
                                              Kh[sub * 64: sub * 64 + 64,
                                                 chalf + kc * 128: chalf + kc * 128 + 128],
                                              qTg[sub * 64: sub * 64 + 64, hp * 512 + q0:hp * 512 + 512],
                                              start=True, stop=True, inc="pe"))
                            if sub == 1 and pi > 0:
                                emit_pv()
                                emit_pv()
                            gates = []
                            for half in range(2):
                                q0e = max(0, (2 * pi + half) - 4 * g) * 128
                                W("scalar", "pe", mms[half])
                                W("scalar", "pe", probs_user[ps])
                                ex = ACT(probsT[:, ps * 1024 + half * 512 + q0e: ps * 1024 + half * 512 + 512],
                                         pbank[:, half * 512 + q0e:half * 512 + 512],
                                         AF.Exp, bias=statc(C_NEGC), scale=0.125)
                                half_free[2 * b3 + half] = ("act", ex)
                                gate = ("act", ex)
                                if pi >= 2 * g:
                                    o = pi - 2 * g
                                    W("vector", "act", ex)
                                    gv = OP("vector", lambda e, p_=ps, oo=o, hf=half, q_=q0e:
                                            e.tensor_mul(probsT[:, p_ * 1024 + hf * 512 + q_:p_ * 1024 + hf * 512 + 512],
                                                         probsT[:, p_ * 1024 + hf * 512 + q_:p_ * 1024 + hf * 512 + 512],
                                                         masks2[:, oo * 1024 + hf * 512 + q_:oo * 1024 + hf * 512 + 512]), inc="dve")
                                    gate = ("dve", gv)
                                gates.append(gate)
                            pend.append((hh, pvb_, ps, pi, gates[0], gates[1]))
                        if pi == 0:
                            emit_norm()
                    emit_pv()
                    emit_pv()
                    Kh_user[ks0] = pv_done[(h1, pairs - 1)]
                    Kh_user[ks0 + 1] = pv_done[(h2, pairs - 1)]
                    if g < 4:
                        if 2 * hp + 5 < p2_h:
                            p2_load_Kh(g, 2 * hp + 4)
                            p2_load_Kh(g, 2 * hp + 5)
                    elif hp + 1 < p2_h // 2:
                        p2_load_Kh(g, h2 + 1)
                        p2_load_Kh(g, h2 + 2)
                    pending_norm.append((h1, pE, pv_done[(h1, pairs - 1)]))
                    pending_norm.append((h2, pF, pv_done[(h2, pairs - 1)]))
                    if pend_wo:
                        pend_wo.pop(0)()
                emit_norm()
                if pend_wo:
                    pend_wo.pop(0)()
                qTg_user = CNT["pe"]

                if g + 1 < p2_g:
                    emit_qproj(g + 1)
                    for h0 in range(4 if g + 1 < 4 else 2):
                        p2_load_Kh(g + 1, h0)

                def mk_wo_closures(gw, nd0, nd1):
                    evs = {}

                    def stageA(dc):
                        hoff = (gw % 2) * 4096
                        bs = 2 * (dc % 2)
                        if dc == 0:
                            WD("tensor", "wo")
                        W("tensor", "dve", nd0)
                        W("tensor", "dve", nd1)
                        WR("tensor", half_free[bs])
                        mwo = None
                        for ec in range(NDC):
                            mwo = MM(HS[bs], woTb[:, ec * 1024 + dc * 128: ec * 1024 + dc * 128 + 128],
                                     headsT[:, hoff + ec * 512: hoff + (ec + 1) * 512],
                                     start=(ec == 0), stop=(ec == 7), inc="pe" if ec == 7 else None)
                        headsT_user[gw % 2] = mwo
                        W("vector", "pe", mwo)
                        W("vector", "pe", ao_user[dc % 2])
                        ev2 = OP("vector", lambda e, sl=dc % 2, b=HS[bs]:
                                 e.tensor_copy(aob[:, sl * 512: sl * 512 + 512], b), inc="dve")
                        half_free[bs] = ("dve", ev2)
                        evs[dc] = ev2

                    def stageB(dc):
                        nonlocal pG_free, xing_user
                        W("tensor", "dve", evs[dc])
                        WR("tensor", pG_free)
                        ttr = None
                        for j in range(4):
                            ttr = TRx(pG[:, j * 128:(j + 1) * 128],
                                      aob[:, (dc % 2) * 512 + j * 128: (dc % 2) * 512 + j * 128 + 128],
                                      idnf[:, :], inc="pe" if j == 3 else None)
                        ao_user[dc % 2] = ttr
                        W("vector", "pe", ttr)
                        if dc == 0:
                            WD("vector", "xi")
                            WD("vector", "x2s")
                        adds_done = OP("vector", lambda e, d=dc:
                                       e.tensor_add(bass.AP(x2b, d * 128, [[4096, 128], [1024, 4], [1, 128]]),
                                                    bass.AP(pG, 0, [[512, 128], [128, 4], [1, 128]]),
                                                    bass.AP(xing, d * 128, [[4096, 128], [1024, 4], [1, 128]])),
                                       inc="dve")
                        pG_free = ("dve", adds_done)
                        if dc == NDC - 1:
                            xing_user = adds_done
                            W("sync", "dve", adds_done)
                            DMA("sync", "x2s",
                                bass.AP(x2_d, gw * 512 * D, [[D, 128], [128 * D, 4], [1, 1024]]),
                                bass.AP(x2b, 0, [[4096, 128], [1024, 4], [1, 1024]]))
                            if gw + 1 < p2_g:
                                WD("sync", "x2s")
                                DMA("sync", "xi",
                                    bass.AP(xing, 0, [[4096, 128], [1024, 4], [1, 1024]]),
                                    bass.AP(x_d, (gw + 1) * 512 * D, [[D, 128], [128 * D, 4], [1, 1024]]))

                    def mk(dc):
                        def c():
                            if dc > 0:
                                stageB(dc - 1)
                            stageA(dc)
                        return c
                    return [mk(dc) for dc in range(NDC)] + [lambda: stageB(NDC - 1)]

                pend_wo.extend(mk_wo_closures(g, norm_done[0], norm_done[1]))
                if g + 1 == p2_g:
                    while pend_wo:
                        pend_wo.pop(0)()

            quiesce()
            flush_block()
        S12.close()

        with contextlib.ExitStack() as S3:
            x2in = S3.enter_context(nc.sbuf_tensor("x2in", [128, 2 * 4096], F32))
            xsq3 = S3.enter_context(nc.sbuf_tensor("xsq3", [128, 1024], F32))
            xlt3 = S3.enter_context(nc.sbuf_tensor("xlt3", [128, 4096], BF16))
            xl2T = S3.enter_context(nc.sbuf_tensor("xl2T", [128, 2 * 4096], BF16))
            hT = S3.enter_context(nc.sbuf_tensor("hT", [128, NHC * 512], BF16))
            w1b = S3.enter_context(nc.sbuf_tensor("w1b", [128, 4 * 1024], BF16))
            w2b = S3.enter_context(nc.sbuf_tensor("w2b", [128, 4 * 4096], BF16))
            a1r1b = S3.enter_context(nc.sbuf_tensor("a1r1b", [128, NDC * 27], BF16))
            a2r2b = S3.enter_context(nc.sbuf_tensor("a2r2b", [128, NHC * 27], BF16))
            b1ab = S3.enter_context(nc.sbuf_tensor("b1ab", [24, FH], BF16))
            b2ab = S3.enter_context(nc.sbuf_tensor("b2ab", [24, D], BF16))
            b1cs = S3.enter_context(nc.sbuf_tensor("b1cs", [128, NHC], F32))
            b2cs = S3.enter_context(nc.sbuf_tensor("b2cs", [128, NDC], F32))
            zg1s = S3.enter_context(nc.sbuf_tensor("zg1s", [27, 512], F32))
            zts = S3.enter_context(nc.sbuf_tensor("zts", [128, 108], F32))
            gex = S3.enter_context(nc.sbuf_tensor("gex", [128, 12], F32))
            zgt = S3.enter_context(nc.sbuf_tensor("zgt", [128, 96], BF16))
            zg1T = S3.enter_context(nc.sbuf_tensor("zg1T", [24, 512], BF16))
            zg2T = S3.enter_context(nc.sbuf_tensor("zg2T", [24, 512], BF16))
            aob3 = S3.enter_context(nc.sbuf_tensor("aob3", [128, 2 * 512], F32))
            outb = S3.enter_context(nc.sbuf_tensor("outb", [128, 4096], F32))

            DMA("gpsimd", "sm", a1r1b.ap(), a1r1_d.ap())
            DMA("gpsimd", "sm", a2r2b.ap(), a2r2_d.ap())
            DMA("gpsimd", "sm", b1ab.ap(), b1a_d.ap())
            DMA("gpsimd", "sm", b2ab.ap(), b2a_d.ap())
            DMA("sync", "smc", b1cs.ap(), b1c_d.ap())
            DMA("sync", "smc", b2cs.ap(), b2c_d.ap())
            WD("scalar", "smc")
            WD("vector", "smc")

            x2_user = [NONE, NONE]
            x2_user_a = [0, 0]
            w1_user = [0, 0, 0, 0]
            w2_user = [0, 0, 0, 0]
            hT_user = 0
            xl2T_user = [0, 0]
            relu_done = [0, 0]
            obank_free = [0, 0]
            ao3_user = [0, 0]
            zgt_seq = 0

            def p3_load_x2(sc):
                slot = sc % 2
                WR("sync", x2_user[slot])
                W("sync", "act", x2_user_a[slot])
                DMA("sync", "x2l%d" % slot,
                    bass.AP(x2in, slot * 4096, [[2 * 4096, 128], [1024, 4], [1, 1024]]),
                    bass.AP(x2_d, sc * 512 * D, [[D, 128], [128 * D, 4], [1, 1024]]))

            def p3_load_w1(flat):
                sc_, hc_ = divmod(flat, NHC)
                if sc_ >= p3_sc:
                    return
                slot = flat % 4
                W("gpsimd", "pe", w1_user[slot])
                DMA("gpsimd", "w1%d" % slot,
                    bass.AP(w1b, slot * 1024, [[4 * 1024, 128], [1, 1024]]),
                    bass.AP(w1bf_d, hc_ * 128 * 1024, [[1024, 128], [1, 1024]]))

            def p3_load_w2(f):
                if f >= p3_sc * NDC:
                    return
                dc_ = f % NDC
                slot = f % 4
                W("scalar", "pe", w2_user[slot])
                DMA("scalar", "w2%d" % slot,
                    bass.AP(w2b, slot * 4096, [[4 * 4096, 128], [1, 4096]]),
                    bass.AP(w2bf_d, dc_ * 128 * 4096, [[4096, 128], [1, 4096]]))

            p3_load_x2(0)
            if p3_sc > 1:
                p3_load_x2(1)
            for f in range(min(4, p3_sc * NHC)):
                p3_load_w1(f)

            def gates_and_zg(src_T_done):
                nonlocal pG_free, pF_free
                ev3 = ACT(zg1s[:, :], pG[0:27, :], AF.Copy)
                pG_free = ("act", ev3)
                W("tensor", "act", ev3)
                WR("tensor", pF_free)
                tz = None
                for j in range(4):
                    tz = TRx(pF[:, j * 128: j * 128 + 27], zg1s[:, j * 128:(j + 1) * 128],
                             idnf[0:27, 0:27], inc="pe" if j == 3 else None)
                W("vector", "pe", tz)
                dzt = OP("vector", lambda e,
                         i=bass.AP(pF, 0, [[512, 128], [128, 4], [1, 27]]),
                         o=bass.AP(zts, 0, [[108, 128], [27, 4], [1, 27]]):
                         e.tensor_copy(o, i), inc="dve")
                pF_free = ("dve", dzt)
                last = dzt
                for sub in range(4):
                    W("vector", "dve", last)
                    dn = OP("vector", lambda e, s=sub:
                            e.reduce_max(statc(C_NGMX), zts[:, s * 27 + 24: s * 27 + 27], AX.X, negate=True), inc="dve")
                    W("scalar", "dve", dn)
                    ag = ACT(gex[:, sub * 3:(sub + 1) * 3], zts[:, sub * 27 + 24: sub * 27 + 27],
                             AF.Exp, bias=statc(C_NGMX))
                    W("vector", "act", ag)
                    ds = OP("vector", lambda e, s=sub:
                            e.reduce_sum(statc(C_GSUM), gex[:, s * 3:(s + 1) * 3], AX.X), inc="dve")
                    W("vector", "dve", ds)
                    dr = OP("vector", lambda e: e.reciprocal(statc(C_GSC), statc(C_GSUM)), inc="dve")
                    W("vector", "dve", dr)
                    dsc = OP("vector", lambda e: e.tensor_scalar(statc(C_GSC), statc(C_GSC), 0.125, None, op0=ALU.mult), inc="dve")
                    W("vector", "dve", dsc)
                    for ee in range(3):
                        last = OP("vector", lambda e, s=sub, x=ee:
                                  e.tensor_scalar(zgt[:, s * 24 + x * 8: s * 24 + x * 8 + 8],
                                                  zts[:, s * 27 + x * 8: s * 27 + x * 8 + 8],
                                                  gex[:, s * 3 + x: s * 3 + x + 1],
                                                  statc(C_GSC), op0=ALU.mult, op1=ALU.mult), inc="dve")
                return last

            def zg_transpose_out(dst, dzgt):
                nonlocal pT_free
                W("tensor", "dve", dzgt)
                WR("tensor", pT_free)
                tz2 = None
                for j in range(4):
                    tz2 = TRx(pT[0:24, j * 128:(j + 1) * 128], zgt[:, j * 24:(j + 1) * 24],
                              idnb[:, :], inc="pe" if j == 3 else None)
                W("vector", "pe", tz2)
                ev = OP("vector", lambda e,
                        i=bass.AP(pT, 0, [[1024, 24], [128, 4], [1, 128]]),
                        o=bass.AP(dst, 0, [[512, 24], [128, 4], [1, 128]]):
                        e.tensor_copy(o, i), inc="dve")
                pT_free = ("dve", ev)
                return ev

            p3_tr_last = [0]
            xl2T_ready = [0, 0]

            ln2_dxls = {}

            def emit_ln2A(sc, sub):
                slot = sc % 2
                if sub == 0:
                    WD("vector", "x2l%d" % slot)
                    WD("scalar", "x2l%d" % slot)
                    W("vector", "pe", p3_tr_last[0])
                    ln2_dxls[sc] = []
                xi = bass.AP(x2in, slot * 4096 + sub * 1024, [[2 * 4096, 128], [1, 1024]])
                dxl, asq = ln_chain("vector", "scalar", xi, xsq3,
                                    xlt3[:, sub * 1024:(sub + 1) * 1024])
                ln2_dxls[sc].append(dxl)
                if sub == 3:
                    x2_user_a[slot] = asq

            def emit_ln2B(sc):
                nonlocal pT_free
                slot = sc % 2
                dxls = ln2_dxls.pop(sc)
                for sub in range(4):
                    W("tensor", "dve", dxls[sub])
                    WR("tensor", pT_free)
                    tr = None
                    for dc in range(NDC):
                        tr = TRx(pT[:, dc * 128:(dc + 1) * 128],
                                 xlt3[:, sub * 1024 + dc * 128: sub * 1024 + dc * 128 + 128],
                                 idnb[:, :], inc="pe" if dc == 7 else None)
                    W("vector", "pe", tr)
                    if sub == 0:
                        W("vector", "pe", xl2T_user[slot])
                    ev = OP("vector", lambda e,
                            i=bass.AP(pT, 0, [[1024, 128], [128, NDC], [1, 128]]),
                            o=bass.AP(xl2T, slot * 4096 + sub * 128, [[2 * 4096, 128], [512, NDC], [1, 128]]):
                            e.tensor_copy(o, i), inc="dve")
                    pT_free = ("dve", ev)
                p3_tr_last[0] = tr
                xl2T_ready[slot] = ev

            for _sub in range(4):
                emit_ln2A(0, _sub)
            emit_ln2B(0)
            z1_res = {}

            def emit_z1(zc):
                zslot = zc % 2
                WD("tensor", "sm")
                W("tensor", "dve", xl2T_ready[zslot])
                WR("tensor", pG_free)
                mz = None
                for dc in range(NDC):
                    mz = MM(pG[0:27, :], a1r1b[:, dc * 27:(dc + 1) * 27],
                            bass.AP(xl2T, zslot * 4096 + dc * 512, [[2 * 4096, 128], [1, 512]]),
                            start=(dc == 0), stop=(dc == 7), inc="pe" if dc == 7 else None)
                W("scalar", "pe", mz)
                z1_res[zc] = gates_and_zg(mz)

            emit_z1(0)
            for sc in range(p3_sc):
                slot = sc % 2
                dzgt = z1_res.pop(sc)
                dzg1T = None

                if sc > 0:
                    W("scalar", "pe", hT_user)
                for hc in range(NHC):
                    flat = sc * NHC + hc
                    wslot = flat % 4
                    bs = 2 * (hc % 2)
                    WD("tensor", "w1%d" % wslot)
                    WR("tensor", half_free[bs])
                    for dc in range(NDC):
                        MM(HS[bs], w1b[:, wslot * 1024 + dc * 128: wslot * 1024 + dc * 128 + 128],
                           bass.AP(xl2T, slot * 4096 + dc * 512, [[2 * 4096, 128], [1, 512]]),
                           start=(dc == 0), stop=False, inc=None)
                    if hc == 0:
                        dzg1T = zg_transpose_out(zg1T, dzgt)
                        W("tensor", "dve", dzg1T)
                    mh = MM(HS[bs], b1ab[0:24, hc * 128:(hc + 1) * 128], zg1T[0:24, :],
                            start=False, stop=True, inc="pe")
                    w1_user[wslot] = mh
                    p3_load_w1(flat + 4)
                    W("scalar", "pe", mh)
                    rl = ACT(hT[:, hc * 512:(hc + 1) * 512], HS[bs], AF.Relu,
                             bias=b1cs[:, hc:hc + 1])
                    half_free[bs] = ("act", rl)
                    relu_done[hc % 2] = rl
                    if hc in (8, 11, 14, 17) and sc + 1 < p3_sc:
                        emit_ln2A(sc + 1, (hc - 8) // 3)
                    if hc == 15:
                        W("tensor", "act", rl)
                        WR("tensor", pG_free)
                        for hz in range(NHC // 2):
                            MM(pG[0:27, :], a2r2b[:, hz * 27:(hz + 1) * 27],
                               hT[:, hz * 512:(hz + 1) * 512],
                               start=(hz == 0), stop=False, inc=None)
                xl2T_user[slot] = CNT["pe"]
                relu_all = CNT["act"]

                if sc == 0:
                    p3_load_w2(0)
                    p3_load_w2(1)
                    p3_load_w2(2)
                    p3_load_w2(3)
                if sc + 1 < p3_sc:
                    emit_ln2B(sc + 1)

                W("tensor", "act", relu_all)
                mz2 = None
                for hc in range(NHC // 2, NHC):
                    mz2 = MM(pG[0:27, :], a2r2b[:, hc * 27:(hc + 1) * 27],
                             hT[:, hc * 512:(hc + 1) * 512],
                             start=False, stop=(hc == NHC - 1), inc="pe" if hc == NHC - 1 else None)
                W("scalar", "pe", mz2)
                dzgt2 = gates_and_zg(mz2)
                dzg2T = None

                pend_tr = []

                def emit_otr():
                    nonlocal pF_free
                    if not pend_tr:
                        return None
                    d_, dev_ = pend_tr.pop(0)
                    W("tensor", "dve", dev_)
                    WR("tensor", pF_free)
                    to = None
                    for j in range(4):
                        to = TRx(pF[:, j * 128:(j + 1) * 128],
                                 aob3[:, (d_ % 2) * 512 + j * 128: (d_ % 2) * 512 + j * 128 + 128],
                                 idnf[:, :], inc="pe" if j == 3 else None)
                    ao3_user[d_ % 2] = to
                    W("vector", "pe", to)
                    if d_ == 0:
                        WD("vector", "o")
                    adds3 = OP("vector", lambda e, d=d_, sl=slot:
                               e.tensor_add(bass.AP(outb, d * 128, [[4096, 128], [1024, 4], [1, 128]]),
                                            bass.AP(pF, 0, [[512, 128], [128, 4], [1, 128]]),
                                            bass.AP(x2in, sl * 4096 + d * 128, [[2 * 4096, 128], [1024, 4], [1, 128]])),
                               inc="dve")
                    pF_free = ("dve", adds3)
                    return adds3

                for dc in range(NDC):
                    w2slot = (sc * NDC + dc) % 4
                    bs = 2 * (dc % 2) + 1
                    WD("tensor", "w2%d" % w2slot)
                    WR("tensor", half_free[bs])
                    for hc in range(NHC):
                        MM(HS[bs], w2b[:, w2slot * 4096 + hc * 128: w2slot * 4096 + hc * 128 + 128],
                           hT[:, hc * 512:(hc + 1) * 512],
                           start=(hc == 0), stop=False, inc=None)
                    if dc == 0:
                        dzg2T = zg_transpose_out(zg2T, dzgt2)
                        W("tensor", "dve", dzg2T)
                    mo = MM(HS[bs], b2ab[0:24, dc * 128:(dc + 1) * 128], zg2T[0:24, :],
                            start=False, stop=True, inc="pe")
                    w2_user[w2slot] = mo
                    p3_load_w2(sc * NDC + dc + 4)
                    if dc == 4 and sc + 1 < p3_sc:
                        emit_z1(sc + 1)
                    W("vector", "pe", mo)
                    W("vector", "pe", ao3_user[dc % 2])
                    dev = OP("vector", lambda e, d=dc, b=HS[bs], sl=dc % 2:
                             e.tensor_scalar(aob3[:, sl * 512: sl * 512 + 512], b,
                                             b2cs[:, d:d + 1], None, op0=ALU.add), inc="dve")
                    half_free[bs] = ("dve", dev)
                    obank_free[dc % 2] = dev
                    emit_otr()
                    pend_tr.append((dc, dev))
                adds3 = emit_otr()
                hT_user = CNT["pe"]
                x2_user[slot] = ("dve", adds3)
                W("sync", "dve", adds3)
                DMA("sync", "o",
                    bass.AP(y_d, sc * 512 * D, [[D, 128], [128 * D, 4], [1, 1024]]),
                    bass.AP(outb, 0, [[4096, 128], [1024, 4], [1, 1024]]))
                if sc + 2 < p3_sc:
                    p3_load_x2(sc + 2)

            quiesce()
            flush_block()

    return nc


def _host_prep(ins):
    f32 = np.float32
    x = np.ascontiguousarray(np.asarray(ins["L0_attn_x"], dtype=f32))
    wqkv = np.asarray(ins["L0_attn_W_qkv"], dtype=f32)
    wo = np.asarray(ins["L0_attn_W_o"], dtype=f32)
    w1 = np.asarray(ins["L0_ffn_W1"], dtype=f32)
    b1 = np.asarray(ins["L0_ffn_b1"], dtype=f32)
    a1 = np.asarray(ins["L0_ffn_A1"], dtype=f32)
    B1 = np.asarray(ins["L0_ffn_B1"], dtype=f32)
    r1 = np.asarray(ins["L0_ffn_R1"], dtype=f32)
    w2 = np.asarray(ins["L0_ffn_W2"], dtype=f32)
    b2 = np.asarray(ins["L0_ffn_b2"], dtype=f32)
    a2 = np.asarray(ins["L0_ffn_A2"], dtype=f32)
    B2 = np.asarray(ins["L0_ffn_B2"], dtype=f32)
    r2 = np.asarray(ins["L0_ffn_R2"], dtype=f32)

    d = {}
    d["wqkvT"] = np.ascontiguousarray(wqkv.T)
    d["woT"] = np.ascontiguousarray(wo.T)
    d["w1r"] = np.ascontiguousarray(
        w1.reshape(8, 128, 32, 128).transpose(2, 1, 0, 3)).reshape(32, 128, 1024)
    d["w2r"] = np.ascontiguousarray(
        w2.reshape(32, 128, 8, 128).transpose(2, 1, 0, 3)).reshape(8, 128, 4096)
    a1all = a1.transpose(1, 0, 2).reshape(1024, 24)
    a2all = a2.transpose(1, 0, 2).reshape(4096, 24)
    a1r1 = np.zeros((128, 8 * 27), f32)
    for dc in range(8):
        a1r1[:, dc * 27:dc * 27 + 24] = a1all[dc * 128:(dc + 1) * 128, :]
        a1r1[:, dc * 27 + 24:dc * 27 + 27] = r1[dc * 128:(dc + 1) * 128, :]
    a2r2 = np.zeros((128, 32 * 27), f32)
    for hc in range(32):
        a2r2[:, hc * 27:hc * 27 + 24] = a2all[hc * 128:(hc + 1) * 128, :]
        a2r2[:, hc * 27 + 24:hc * 27 + 27] = r2[hc * 128:(hc + 1) * 128, :]
    d["a1r1"] = a1r1
    d["a2r2"] = a2r2
    d["b1a"] = np.ascontiguousarray(B1.reshape(24, 4096))
    d["b2a"] = np.ascontiguousarray(B2.reshape(24, 1024))
    d["b1c"] = np.ascontiguousarray(b1.reshape(32, 128).T)
    d["b2c"] = np.ascontiguousarray(b2.reshape(8, 128).T)
    d["idnb"] = np.eye(128, dtype=ml_dtypes.bfloat16)
    d["idnf"] = np.eye(128, dtype=f32)
    o2 = np.zeros((128, 2), f32)
    o2[0:64, 0] = 1.0
    o2[64:128, 1] = 1.0
    d["ones2"] = o2.astype(ml_dtypes.bfloat16)
    d["ones64r"] = np.ones((1, 64), f32)
    d["ones128c"] = np.ones((128, 1), ml_dtypes.bfloat16)
    d["ones128r"] = np.ones((1, 128), f32)
    kr = np.arange(128)[:, None]
    qc = np.arange(512)[None, :]
    m = np.concatenate([(qc >= kr + 128 * j).astype(f32) for j in range(4)], axis=1)
    d["masks"] = m.astype(ml_dtypes.bfloat16)
    return x, d


def kernel(**ins):
    x, shared = _host_prep(ins)
    nc = build_nc()
    in_maps = []
    for b in range(8):
        m = dict(shared)
        m["x"] = np.ascontiguousarray(x[b])
        in_maps.append(m)
    res = run_bass_kernel_spmd(nc, in_maps, core_ids=list(range(8)))
    return np.stack([np.asarray(res.results[b]["y"]) for b in range(8)], axis=0)


# version 48
# speedup vs baseline: 1.2070x; 1.0023x over previous
import contextlib
import numpy as np
import ml_dtypes
import concourse.bass as bass
import concourse.mybir as mybir
from concourse.bass_utils import run_bass_kernel_spmd

F32 = mybir.dt.float32
BF16 = mybir.dt.bfloat16
AF = mybir.ActivationFunctionType
ALU = mybir.AluOpType
AX = mybir.AxisListType

D = 1024
S = 4096
FH = 4096
NDC = 8
NHC = 32
VW = 1040
EPS = 1e-5
LNC = float(np.log(0.125 * 1.04))


def build_nc(p1_sc=8, p2_g=8, p2_h=16, p3_sc=8, debug=False):
    nc = bass.Bass("TRN2", target_bir_lowering=False)

    x_d = nc.dram_tensor("x", [S, D], F32, kind="ExternalInput")
    wqkvT_d = nc.dram_tensor("wqkvT", [D, 3 * D], F32, kind="ExternalInput")
    woT_d = nc.dram_tensor("woT", [D, D], F32, kind="ExternalInput")
    w1r_d = nc.dram_tensor("w1r", [NHC, 128, NDC * 128], F32, kind="ExternalInput")
    w2r_d = nc.dram_tensor("w2r", [NDC, 128, NHC * 128], F32, kind="ExternalInput")
    a1r1_d = nc.dram_tensor("a1r1", [128, NDC * 27], F32, kind="ExternalInput")
    a2r2_d = nc.dram_tensor("a2r2", [128, NHC * 27], F32, kind="ExternalInput")
    b1a_d = nc.dram_tensor("b1a", [24, FH], F32, kind="ExternalInput")
    b2a_d = nc.dram_tensor("b2a", [24, D], F32, kind="ExternalInput")
    b1c_d = nc.dram_tensor("b1c", [128, NHC], F32, kind="ExternalInput")
    b2c_d = nc.dram_tensor("b2c", [128, NDC], F32, kind="ExternalInput")
    idnb_d = nc.dram_tensor("idnb", [128, 128], BF16, kind="ExternalInput")
    idnf_d = nc.dram_tensor("idnf", [128, 128], F32, kind="ExternalInput")
    ones2_d = nc.dram_tensor("ones2", [128, 2], BF16, kind="ExternalInput")
    ones64r_d = nc.dram_tensor("ones64r", [1, 64], F32, kind="ExternalInput")
    ones128r_d = nc.dram_tensor("ones128r", [1, 128], F32, kind="ExternalInput")
    ones128c_d = nc.dram_tensor("ones128c", [128, 1], BF16, kind="ExternalInput")
    masks_d = nc.dram_tensor("masks", [128, 2048], BF16, kind="ExternalInput")

    y_d = nc.dram_tensor("y", [S, D], F32, kind="ExternalOutput")
    sk = "ExternalOutput" if debug else "Internal"
    xlnT_d = nc.dram_tensor("xlnT", [D, S], BF16, kind=sk)
    kT_d = nc.dram_tensor("kT", [D, S], BF16, kind=sk)
    x2_d = nc.dram_tensor("x2", [S, D], F32, kind=sk)
    w1bf_d = nc.dram_tensor("w1bf", [NHC, 128, NDC * 128], BF16, kind="Internal")
    w2bf_d = nc.dram_tensor("w2bf", [NDC, 128, NHC * 128], BF16, kind="Internal")

    EM = {k: [] for k in ("sync", "gpsimd", "tensor", "scalar", "vector")}
    SYNC_SEMS = ["c", "x0", "x1", "xlt0", "xlt1", "kv0", "kv1", "kv2",
                 "xg0", "xg1", "kh0", "kh1", "kh2", "kh3", "xi", "x2s",
                 "x2l0", "x2l1", "smc", "o"]
    GP_SEMS = ["wq", "wkv", "wo", "sm", "wc", "wcs", "w10", "w11", "w12", "w13",
               "w20", "w21", "w22", "w23"]
    CNT = {s: 0 for s in ["pe", "act", "dve"] + SYNC_SEMS + GP_SEMS}
    SEMH = {}

    def q(qn, fn):
        EM[qn].append(fn)

    WHW = {k: {} for k in EM}

    def W(qn, sem, val):
        if val is None or val <= 0:
            return
        if WHW[qn].get(sem, 0) >= val:
            return
        WHW[qn][sem] = val
        q(qn, lambda e, s=sem, v=val: e.wait_ge(SEMH[s], v))

    def WR(qn, res):
        W(qn, res[0], res[1])

    def WD(qn, sem):
        W(qn, sem, CNT[sem])

    def quiesce():
        for s in SYNC_SEMS:
            WD("sync", s)
        for s in GP_SEMS:
            WD("gpsimd", s)
            WD("sync", s)

    def OP(qn, build, inc=None, by=1):
        if inc is None:
            q(qn, lambda e, b=build: b(e))
            return None
        CNT[inc] += by
        q(qn, lambda e, b=build, s=inc, n=by: b(e).then_inc(SEMH[s], n))
        return CNT[inc]

    def DMA(qn, sem, out_ap, in_ap):
        return OP(qn, lambda e, o=out_ap, i=in_ap: e.dma_start(o, i), inc=sem, by=16)

    def MM(out_ap, lhsT, rhs, start, stop, inc=None):
        return OP("tensor",
                  lambda e, o=out_ap, l=lhsT, r=rhs, st=start, sp=stop:
                  e.matmul(o, l, r, start=st, stop=sp), inc=inc)

    def TRx(out_ap, in_ap, idn, inc=None):
        return OP("tensor",
                  lambda e, o=out_ap, i=in_ap, d=idn: e.transpose(o, i, d), inc=inc)

    def ACT(out_ap, in_ap, func, bias=0.0, scale=1.0):
        return OP("scalar",
                  lambda e, o=out_ap, i=in_ap, f=func, b=bias, s=scale:
                  e.activation(o, i, f, bias=b, scale=s), inc="act")

    def flush_block():
        with nc.Block() as blk:
            for name, dec in (("sync", blk.sync), ("gpsimd", blk.gpsimd),
                              ("tensor", blk.tensor), ("scalar", blk.scalar),
                              ("vector", blk.vector)):
                fns = EM[name]
                EM[name] = []
                if fns:
                    def body(eng, fns=fns):
                        for f in fns:
                            f(eng)
                    dec(body)

    NONE = ("pe", 0)

    with contextlib.ExitStack() as ES:
        for s in CNT:
            SEMH[s] = ES.enter_context(nc.semaphore(s))
        pAB = ES.enter_context(nc.psum_tensor("pAB", [128, 1024], F32))
        pCD = ES.enter_context(nc.psum_tensor("pCD", [128, 1024], F32))
        pE = ES.enter_context(nc.psum_tensor("pE", [128, 512], F32))
        pF = ES.enter_context(nc.psum_tensor("pF", [128, 512], F32))
        pG = ES.enter_context(nc.psum_tensor("pG", [128, 512], F32))
        pT = ES.enter_context(nc.psum_tensor("pT", [128, 1024], BF16))
        HS = [pAB[:, 0:512], pAB[:, 512:1024], pCD[:, 0:512], pCD[:, 512:1024]]
        half_free = [NONE, NONE, NONE, NONE]
        pE_free = NONE
        pF_free = NONE
        pG_free = NONE
        pT_free = NONE

        idnb = ES.enter_context(nc.sbuf_tensor("idnb_s", [128, 128], BF16))
        idnf = ES.enter_context(nc.sbuf_tensor("idnf_s", [128, 128], F32))
        ones2 = ES.enter_context(nc.sbuf_tensor("ones2_s", [128, 2], BF16))
        ones64r = ES.enter_context(nc.sbuf_tensor("ones64r_s", [1, 64], F32))
        ones128r = ES.enter_context(nc.sbuf_tensor("ones128r_s", [1, 128], F32))
        ones128c = ES.enter_context(nc.sbuf_tensor("ones128c_s", [128, 1], BF16))
        stat = ES.enter_context(nc.sbuf_tensor("stat", [128, 32], F32))
        srow = ES.enter_context(nc.sbuf_tensor("srow", [1, 16], F32))
        rowb = ES.enter_context(nc.sbuf_tensor("rowb", [1, 1024], F32))
        wqTb = ES.enter_context(nc.sbuf_tensor("wqTb", [128, NDC * 1024], BF16))
        knmc = ES.enter_context(nc.sbuf_tensor("knmc", [2, 64], F32))

        C_S1, C_S2, C_NM, C_VE, C_M2, C_VAR, C_RSTD, C_NMR, C_NEGC, C_NGMX, C_GSUM, C_GSC = range(12)
        KMX_COL = 16
        QMX_COL = 17
        R_MK2, R_LNMK, R_MQ2, R_LNQ, R_LSUM, R_CG, R_NEGC1 = range(7)

        def statc(i):
            return stat[:, i:i + 1]

        def srowc(i):
            return srow[0:1, i:i + 1]

        xsq_user = [0]

        def ln_chain(qv, qa, xi, xsq_t, out_ap):
            OP(qv, lambda e, i=xi: e.reduce_sum(statc(C_S1), i, AX.X), inc="dve")
            W(qa, "dve", xsq_user[0])
            a0 = ACT(xsq_t[:, :], xi, AF.Square)
            W(qv, "act", a0)
            d1 = OP(qv, lambda e, t=xsq_t: e.reduce_sum(statc(C_S2), t[:, :], AX.X), inc="dve")
            xsq_user[0] = d1
            W(qv, "dve", d1)
            OP(qv, lambda e: e.tensor_scalar(statc(C_NM), statc(C_S1), -1.0 / D, None, op0=ALU.mult), inc="dve")
            d1b = OP(qv, lambda e: e.tensor_scalar(statc(C_VE), statc(C_S2), 1.0 / D, EPS, op0=ALU.mult, op1=ALU.add), inc="dve")
            W(qv, "dve", d1b)
            d2 = OP(qv, lambda e: e.tensor_mul(statc(C_M2), statc(C_NM), statc(C_NM)), inc="dve")
            W(qv, "dve", d2)
            d3 = OP(qv, lambda e: e.tensor_sub(statc(C_VAR), statc(C_VE), statc(C_M2)), inc="dve")
            W(qa, "dve", d3)
            a1 = ACT(statc(C_VAR), statc(C_VAR), AF.Ln)
            W(qa, "act", a1)
            a2 = ACT(statc(C_RSTD), statc(C_VAR), AF.Exp, scale=-0.5)
            W(qv, "act", a2)
            d4 = OP(qv, lambda e: e.tensor_mul(statc(C_NMR), statc(C_NM), statc(C_RSTD)), inc="dve")
            W(qv, "dve", d4)
            dout = OP(qv, lambda e, i=xi, o=out_ap:
                      e.tensor_scalar(o, i, statc(C_RSTD), statc(C_NMR), op0=ALU.mult, op1=ALU.add), inc="dve")
            return dout, a0

        S12 = contextlib.ExitStack()
        V = S12.enter_context(nc.sbuf_tensor("V", [128, 32 * VW], BF16))
        with contextlib.ExitStack() as S1:
            wkvTb = S1.enter_context(nc.sbuf_tensor("wkvTb", [128, NDC * 2048], BF16))
            xin1 = S1.enter_context(nc.sbuf_tensor("xin1", [128, 2 * 4096], F32))
            xsq = S1.enter_context(nc.sbuf_tensor("xsq", [128, 1024], F32))
            xlt = S1.enter_context(nc.sbuf_tensor("xlt", [128, 4096], BF16))
            xlT = S1.enter_context(nc.sbuf_tensor("xlT", [128, 2 * 4096], BF16))
            kvb = S1.enter_context(nc.sbuf_tensor("kvb", [128, 7 * 512], BF16))
            wstg = S1.enter_context(nc.sbuf_tensor("wstg", [128, 2 * 1024], BF16))
            ksq = S1.enter_context(nc.sbuf_tensor("ksq", [128, 512], BF16))

            DMA("sync", "c", idnb.ap(), idnb_d.ap())
            DMA("sync", "c", idnf.ap(), idnf_d.ap())
            DMA("sync", "c", ones2.ap(), ones2_d.ap())
            DMA("sync", "c", ones64r.ap(), ones64r_d.ap())
            DMA("sync", "c", ones128r.ap(), ones128r_d.ap())
            DMA("sync", "c", ones128c.ap(), ones128c_d.ap())
            DMA("gpsimd", "wq",
                bass.AP(wqTb, 0, [[NDC * 1024, 128], [1024, NDC], [1, 1024]]),
                bass.AP(wqkvT_d, 0, [[3 * D, 128], [128 * 3 * D, NDC], [1, 1024]]))
            DMA("gpsimd", "wkv",
                bass.AP(wkvTb, 0, [[NDC * 2048, 128], [2048, NDC], [1, 2048]]),
                bass.AP(wqkvT_d, 1024, [[3 * D, 128], [128 * 3 * D, NDC], [1, 2048]]))
            for i in range(64):
                wsrc, wdst = (w1r_d, w1bf_d) if i < 32 else (w2r_d, w2bf_d)
                off = (i % 32) * 128 * 1024
                slot = i % 2
                if i >= 2:
                    W("gpsimd", "wcs", 16 * (i - 1))
                ld = DMA("gpsimd", "wc",
                         bass.AP(wstg, slot * 1024, [[2 * 1024, 128], [1, 1024]]),
                         bass.AP(wsrc, off, [[1024, 128], [1, 1024]]))
                W("gpsimd", "wc", ld)
                DMA("gpsimd", "wcs",
                    bass.AP(wdst, off, [[1024, 128], [1, 1024]]),
                    bass.AP(wstg, slot * 1024, [[2 * 1024, 128], [1, 1024]]))
            for i in range(32):
                OP("vector", lambda e, a=bass.AP(V, i * VW + 64, [[32 * VW, 128], [65, 16], [1, 1]]):
                   e.memset(a, 1.0), inc="dve")

            WD("tensor", "c")

            xin_user = [NONE, NONE]
            xin_user_a = [0, 0]
            kvb_sq = [0, 0, 0, 0]
            xlT_user = [0, 0]
            kvb_user = [NONE] * 7
            ksq_user = 0

            def p1_load_x(sc):
                slot = sc % 2
                WR("sync", xin_user[slot])
                W("sync", "act", xin_user_a[slot])
                DMA("sync", "x%d" % slot,
                    bass.AP(xin1, slot * 4096, [[2 * 4096, 128], [1024, 4], [1, 1024]]),
                    bass.AP(x_d, sc * 512 * D, [[D, 128], [128 * D, 4], [1, 1024]]))

            p1_load_x(0)
            if p1_sc > 1:
                p1_load_x(1)

            def p1_transposes(sc, dxls):
                nonlocal pT_free
                slot = sc % 2
                for sub in range(4):
                    W("tensor", "dve", dxls[sub])
                    WR("tensor", pT_free)
                    tr = None
                    for dc in range(NDC):
                        tr = TRx(pT[:, dc * 128:(dc + 1) * 128],
                                 xlt[:, sub * 1024 + dc * 128: sub * 1024 + dc * 128 + 128],
                                 idnb[:, :], inc="pe" if dc == 7 else None)
                    W("vector", "pe", tr)
                    if sub == 0:
                        WD("vector", "xlt%d" % slot)
                        W("vector", "pe", xlT_user[slot])
                    ev = OP("vector", lambda e,
                            i=bass.AP(pT, 0, [[1024, 128], [128, NDC], [1, 128]]),
                            o=bass.AP(xlT, slot * 4096 + sub * 128, [[2 * 4096, 128], [512, NDC], [1, 128]]):
                            e.tensor_copy(o, i), inc="dve")
                    pT_free = ("dve", ev)
                dve_xlT = ev
                W("sync", "dve", dve_xlT)
                DMA("sync", "xlt%d" % slot,
                    bass.AP(xlnT_d, sc * 512, [[S, 128], [128 * S, NDC], [1, 512]]),
                    bass.AP(xlT, slot * 4096, [[2 * 4096, 128], [512, NDC], [1, 512]]))
                return dve_xlT

            dxls_pend = {}

            def p1_ln_sub(sc, sub):
                slot = sc % 2
                if sub == 0:
                    WD("vector", "x%d" % slot)
                    WD("scalar", "x%d" % slot)
                    dxls_pend[sc] = []
                xi = bass.AP(xin1, slot * 4096 + sub * 1024, [[2 * 4096, 128], [1, 1024]])
                dxl, asq = ln_chain("vector", "scalar", xi, xsq,
                                    xlt[:, sub * 1024:(sub + 1) * 1024])
                dxls_pend[sc].append(dxl)
                if sub == 3:
                    xin_user[slot] = ("dve", dxl)
                    xin_user_a[slot] = asq

            def p1_kvproj(sc, dve_xlT, ln_next):
                nonlocal ksq_user, pE_free, pT_free
                slot = sc % 2
                WD("tensor", "wkv")
                for fc in range(16):
                    bs = fc % 4
                    WR("tensor", half_free[bs])
                    if fc == 0:
                        W("tensor", "dve", dve_xlT)
                    mmv = None
                    for dc in range(NDC):
                        mmv = MM(HS[bs], wkvTb[:, dc * 2048 + fc * 128: dc * 2048 + fc * 128 + 128],
                                 bass.AP(xlT, slot * 4096 + dc * 512, [[2 * 4096, 128], [1, 512]]),
                                 start=(dc == 0), stop=(dc == 7), inc="pe" if dc == 7 else None)
                    kslot = fc % 4 if fc < 8 else 4 + fc % 3
                    W("scalar", "pe", mmv)
                    WD("scalar", "kv%d" % (kslot % 3))
                    WR("scalar", kvb_user[kslot])
                    if fc < 8:
                        W("scalar", "dve", kvb_sq[kslot])
                    evk = ACT(kvb[:, kslot * 512:(kslot + 1) * 512], HS[bs], AF.Copy)
                    half_free[bs] = ("act", evk)
                    if fc < 8:
                        W("scalar", "act", evk)
                        DMA("scalar", "kv%d" % (kslot % 3),
                            bass.AP(kT_d, fc * 128 * S + sc * 512, [[S, 128], [1, 512]]),
                            kvb[:, kslot * 512:(kslot + 1) * 512])
                        W("vector", "act", evk)
                        W("vector", "pe", ksq_user)
                        aq = OP("vector", lambda e, s=kvb[:, kslot * 512:(kslot + 1) * 512]:
                                e.tensor_mul(ksq[:, :], s, s), inc="dve")
                        kvb_sq[kslot] = aq
                        W("tensor", "dve", aq)
                        WR("tensor", pE_free)
                        nm = MM(pE[0:2, :], ones2[:, :], ksq[:, :], start=True, stop=True, inc="pe")
                        ksq_user = nm
                        W("vector", "pe", nm)
                        col = sc * 8 + fc
                        dmx = OP("vector", lambda e, c=col:
                                 e.reduce_max(knmc[0:2, c:c + 1], pE[0:2, :], AX.X), inc="dve")
                        pE_free = ("dve", dmx)
                        kvb_user[kslot] = ("kv%d" % (kslot % 3), CNT["kv%d" % (kslot % 3)])
                    else:
                        vfc = fc - 8
                        he = 2 * vfc
                        W("tensor", "act", evk)
                        WR("tensor", pT_free)
                        tr = None
                        for sub in range(4):
                            tr = TRx(pT[:, sub * 128:(sub + 1) * 128],
                                     kvb[:, kslot * 512 + sub * 128: kslot * 512 + sub * 128 + 128],
                                     idnb[:, :], inc="pe" if sub == 3 else None)
                        kvb_user[kslot] = ("pe", tr)
                        W("scalar", "pe", tr)
                        OP("scalar", lambda e,
                           i=bass.AP(pT, 0, [[1024, 128], [128, 4], [1, 64]]),
                           o=bass.AP(V, (sc * 4) * VW + he * 65, [[32 * VW, 128], [VW, 4], [1, 64]]):
                           e.activation(o, i, AF.Copy), inc="act")
                        ev2 = OP("scalar", lambda e,
                                 i=bass.AP(pT, 64, [[1024, 128], [128, 4], [1, 64]]),
                                 o=bass.AP(V, (sc * 4) * VW + (he + 1) * 65, [[32 * VW, 128], [VW, 4], [1, 64]]):
                                 e.activation(o, i, AF.Copy), inc="act")
                        pT_free = ("act", ev2)
                    if ln_next is not None and fc in (9, 11, 13, 15):
                        p1_ln_sub(ln_next, (fc - 9) // 2)
                xlT_user[slot] = CNT["pe"]

            for sub in range(4):
                p1_ln_sub(0, sub)
            for sc in range(p1_sc):
                if sc + 2 < p1_sc:
                    p1_load_x(sc + 2)
                dve_xlT = p1_transposes(sc, dxls_pend.pop(sc))
                p1_kvproj(sc, dve_xlT, sc + 1 if sc + 1 < p1_sc else None)

            W("vector", "dve", CNT["dve"])
            d5 = OP("vector", lambda e, n=8 * p1_sc:
                    e.reduce_max(stat[0:2, KMX_COL:KMX_COL + 1], knmc[0:2, 0:n], AX.X), inc="dve")
            W("tensor", "dve", d5)
            WR("tensor", pE_free)
            t5 = TRx(pE[0:1, 0:2], stat[0:2, KMX_COL:KMX_COL + 1], idnf[0:2, 0:2], inc="pe")
            W("vector", "pe", t5)
            d6 = OP("vector", lambda e: e.reduce_max(srowc(R_MK2), pE[0:1, 0:2], AX.X), inc="dve")
            pE_free = ("dve", d6)
            W("scalar", "dve", d6)
            ACT(srowc(R_LNMK), srowc(R_MK2), AF.Ln)

            quiesce()
            flush_block()

        with contextlib.ExitStack() as S2:
            woTb = S2.enter_context(nc.sbuf_tensor("woTb", [128, NDC * 1024], BF16))
            masks2 = S2.enter_context(nc.sbuf_tensor("masks2", [128, 2048], BF16))
            xlnTg = S2.enter_context(nc.sbuf_tensor("xlnTg", [128, 2 * 4096], BF16))
            qTg = S2.enter_context(nc.sbuf_tensor("qTg", [128, 8 * 512], BF16))
            qsq = S2.enter_context(nc.sbuf_tensor("qsq", [128, 512], BF16))
            Kh = S2.enter_context(nc.sbuf_tensor("Kh", [128, 4096], BF16))
            probsT = S2.enter_context(nc.sbuf_tensor("probsT", [128, 5 * 1024], BF16))
            headsT = S2.enter_context(nc.sbuf_tensor("headsT", [128, 2 * 8 * 512], BF16))
            rb = S2.enter_context(nc.sbuf_tensor("rb", [128, 512], F32))
            rb2 = S2.enter_context(nc.sbuf_tensor("rb2", [128, 512], F32))
            aob = S2.enter_context(nc.sbuf_tensor("aob", [128, 2 * 512], F32))
            xing = S2.enter_context(nc.sbuf_tensor("xing", [128, 4096], F32))
            x2b = S2.enter_context(nc.sbuf_tensor("x2b", [128, 4096], F32))

            DMA("gpsimd", "wo",
                bass.AP(woTb, 0, [[NDC * 1024, 128], [1024, NDC], [1, 1024]]),
                bass.AP(woT_d, 0, [[D, 128], [128 * D, NDC], [1, 1024]]))
            DMA("sync", "c", masks2.ap(), masks_d.ap())
            WD("vector", "c")

            xlnTg_user = [0, 0]
            Kh_user = [0, 0, 0, 0]
            headsT_user = [0, 0]
            pend_wo = []
            xing_user = 0
            qTg_user = 0
            qsq_user = [0, 0]
            norm_done = [0, 0]
            probs_user = [0] * 6
            sq2 = [0]
            rowb_user = 0
            rb_user = 0
            ao_user = [0, 0]

            def p2_load_xlnTg(g):
                slot = g % 2
                W("sync", "pe", xlnTg_user[slot])
                DMA("sync", "xg%d" % slot,
                    bass.AP(xlnTg, slot * 4096, [[2 * 4096, 128], [512, NDC], [1, 512]]),
                    bass.AP(xlnT_d, g * 512, [[S, 128], [128 * S, NDC], [1, 512]]))

            def p2_load_Kh(g, h):
                kmax = 512 * (g + 1)
                if g < 4:
                    slot = h % 4
                    off = (h % 2) * 64 * 4096 + ((h // 2) % 2) * 2048
                    W("sync", "pe", Kh_user[slot])
                    DMA("sync", "kh%d" % slot,
                        bass.AP(Kh, off, [[4096, 64], [1, kmax]]),
                        bass.AP(kT_d, h * 64 * S, [[S, 64], [1, kmax]]))
                else:
                    slot = h % 2
                    W("sync", "pe", max(Kh_user[slot], Kh_user[slot + 2]))
                    DMA("sync", "kh%d" % slot,
                        bass.AP(Kh, slot * 64 * 4096, [[4096, 64], [1, kmax]]),
                        bass.AP(kT_d, h * 64 * S, [[S, 64], [1, kmax]]))

            def emit_qproj(g):
                nonlocal pG_free, qn_last
                gslot = g % 2
                WD("tensor", "wq")
                WD("tensor", "xg%d" % gslot)
                pend_q = []

                def emit_qn(is_last):
                    nonlocal qn_last
                    if not pend_q:
                        return
                    m2_, first_ = pend_q.pop()
                    W("tensor", "dve", m2_)
                    if first_:
                        WR("tensor", pG_free)
                    qn_last = MM(pG[0:1, :], ones128c[:, 0:1], qsq[:, :],
                                 start=first_, stop=is_last, inc="pe")
                    qsq_user[0] = qn_last

                for fc in range(8):
                    bs = fc % 4
                    WR("tensor", half_free[bs])
                    mmq = None
                    for dc in range(NDC):
                        mmq = MM(HS[bs], wqTb[:, dc * 1024 + fc * 128: dc * 1024 + fc * 128 + 128],
                                 bass.AP(xlnTg, gslot * 4096 + dc * 512, [[2 * 4096, 128], [1, 512]]),
                                 start=(dc == 0), stop=(dc == 7), inc="pe" if dc == 7 else None)
                    W("vector", "pe", mmq)
                    if fc == 0:
                        W("vector", "pe", qTg_user)
                    q1 = qTg[:, fc * 512:fc * 512 + 512]
                    c2 = OP("vector", lambda e, o=q1, b=HS[bs]: e.tensor_copy(o, b), inc="dve")
                    half_free[bs] = ("dve", c2)
                    emit_qn(False)
                    W("vector", "dve", c2)
                    W("vector", "pe", qsq_user[0])
                    m2 = OP("vector", lambda e, i=q1: e.tensor_mul(qsq[:, :], i, i), inc="dve")
                    pend_q.append((m2, fc == 0))
                emit_qn(True)
                xlnTg_user[gslot] = CNT["pe"]
                if g + 2 < p2_g:
                    p2_load_xlnTg(g + 2)

                W("vector", "pe", qn_last)
                d8 = OP("vector", lambda e: e.reduce_max(srowc(R_MQ2), pG[0:1, 0:512], AX.X), inc="dve")
                W("scalar", "dve", d8)
                a3 = ACT(srowc(R_LNQ), srowc(R_MQ2), AF.Ln)
                W("vector", "act", a3)
                d9 = OP("vector", lambda e: e.tensor_add(srowc(R_LSUM), srowc(R_LNQ), srowc(R_LNMK)), inc="dve")
                W("scalar", "dve", d9)
                a4 = ACT(srowc(R_CG), srowc(R_LSUM), AF.Exp, scale=0.5, bias=np.float32(LNC))
                W("vector", "act", a4)
                d10 = OP("vector", lambda e: e.tensor_scalar(srowc(R_NEGC1), srowc(R_CG), -1.0, None, op0=ALU.mult), inc="dve")
                W("tensor", "dve", d10)
                t9 = MM(pG[0:128, 3:4], ones128r[0:1, :], srowc(R_NEGC1), start=True, stop=True, inc="pe")
                W("scalar", "pe", t9)
                a5 = ACT(statc(C_NEGC), pG[0:128, 3:4], AF.Copy)
                pG_free = ("act", a5)
                W("scalar", "act", a5)

            qn_last = 0
            p2_load_xlnTg(0)
            if p2_g > 1:
                p2_load_xlnTg(1)
            emit_qproj(0)
            for h0 in range(4):
                p2_load_Kh(0, h0)

            for g in range(p2_g):
                if g == 0:
                    DMA("sync", "xi",
                        bass.AP(xing, 0, [[4096, 128], [1024, 4], [1, 1024]]),
                        bass.AP(x_d, 0, [[D, 128], [128 * D, 4], [1, 1024]]))
                pairs = 2 * g + 2
                pending_norm = []

                def emit_norm():
                    nonlocal pG_free, rowb_user, rb_user
                    if not pending_norm:
                        return
                    ha, pva, pvla = pending_norm.pop(0)
                    hb, pvb2, pvlb = pending_norm.pop(0)
                    W("vector", "pe", max(pvla, pvlb))
                    W("vector", "pe", rowb_user)
                    OP("vector", lambda e: e.tensor_copy(rowb[0:1, 0:512], pva[64:65, :]), inc="dve")
                    cbs = OP("vector", lambda e: e.tensor_copy(rowb[0:1, 512:1024], pvb2[64:65, :]), inc="dve")
                    W("tensor", "dve", cbs)
                    WR("tensor", pG_free)
                    MM(pG[0:64, :], ones64r[0:1, :], rowb[0:1, 0:512],
                       start=True, stop=True)
                    t10 = MM(pG[64:128, :], ones64r[0:1, :], rowb[0:1, 512:1024],
                             start=True, stop=True, inc="pe")
                    rowb_user = t10
                    W("vector", "pe", t10)
                    W("vector", "dve", rb_user)
                    cb = OP("vector", lambda e: e.tensor_copy(rb[:, :], pG[:, :]), inc="dve")
                    pG_free = ("dve", cb)
                    W("vector", "dve", cb)
                    rcp = OP("vector", lambda e: e.reciprocal(rb2[:, :], rb[:, :]), inc="dve")
                    W("vector", "dve", rcp)
                    W("vector", "pe", headsT_user[g % 2])
                    hoff = (g % 2) * 4096
                    mul1 = OP("vector", lambda e, h2=ha, pv=pva, ho=hoff:
                              e.tensor_mul(headsT[0:64, ho + (h2 // 2) * 512: ho + (h2 // 2) * 512 + 512],
                                           pv[0:64, :], rb2[0:64, :]), inc="dve")
                    rb_user = OP("vector", lambda e, h2=hb, pv=pvb2, ho=hoff:
                                 e.tensor_mul(headsT[64:128, ho + (h2 // 2) * 512: ho + (h2 // 2) * 512 + 512],
                                              pv[0:64, :], rb2[64:128, :]), inc="dve")
                    norm_done[0] = mul1
                    norm_done[1] = rb_user

                for hp in range(p2_h // 2):
                    h1, h2 = 2 * hp, 2 * hp + 1
                    ks0 = (2 * hp) % 4 if g < 4 else 0
                    chalf = (hp % 2) * 2048 if g < 4 else 0
                    WD("tensor", "kh%d" % ks0)
                    WD("tensor", "kh%d" % (ks0 + 1))
                    pend = []
                    pv_done = {}

                    def emit_pv():
                        if not pend:
                            return
                        hh, pvb_, ps_, pi_, gate0_, gate1_ = pend.pop(0)
                        if pi_ == 0:
                            W("tensor", "dve", norm_done[hh % 2])
                        mpv_ = None
                        for half in range(2):
                            WR("tensor", gate0_ if half == 0 else gate1_)
                            kc = 2 * pi_ + half
                            q0 = max(0, (kc - 4 * g)) * 128
                            if pi_ == 0 and half == 0:
                                q0 = 0
                            mpv_ = MM(pvb_[0:65, q0:],
                                      V[:, kc * VW + hh * 65: kc * VW + hh * 65 + 65],
                                      probsT[:, ps_ * 1024 + half * 512 + q0: ps_ * 1024 + half * 512 + 512],
                                      start=(pi_ == 0 and half == 0),
                                      stop=(pi_ == pairs - 1 and half == 1),
                                      inc="pe" if half == 1 else None)
                        pv_done[(hh, pi_)] = mpv_
                        probs_user[ps_] = mpv_

                    for pi in range(pairs):
                        for sub, (hh, pvb_) in enumerate(((h1, pE), (h2, pF))):
                            b3 = sq2[0] % 2
                            ps = sq2[0] % 5
                            sq2[0] += 1
                            pbank = pAB if b3 == 0 else pCD
                            mms = []
                            for half in range(2):
                                WR("tensor", half_free[2 * b3 + half])
                                kc = 2 * pi + half
                                q0 = max(0, (kc - 4 * g)) * 128
                                mms.append(MM(pbank[:, half * 512 + q0:half * 512 + 512],
                                              Kh[sub * 64: sub * 64 + 64,
                                                 chalf + kc * 128: chalf + kc * 128 + 128],
                                              qTg[sub * 64: sub * 64 + 64, hp * 512 + q0:hp * 512 + 512],
                                              start=True, stop=True, inc="pe"))
                            if sub == 1 and pi > 0:
                                emit_pv()
                                emit_pv()
                            gates = []
                            for half in range(2):
                                q0e = max(0, (2 * pi + half) - 4 * g) * 128
                                W("scalar", "pe", mms[half])
                                W("scalar", "pe", probs_user[ps])
                                ex = ACT(probsT[:, ps * 1024 + half * 512 + q0e: ps * 1024 + half * 512 + 512],
                                         pbank[:, half * 512 + q0e:half * 512 + 512],
                                         AF.Exp, bias=statc(C_NEGC), scale=0.125)
                                half_free[2 * b3 + half] = ("act", ex)
                                gate = ("act", ex)
                                if pi >= 2 * g:
                                    o = pi - 2 * g
                                    W("vector", "act", ex)
                                    gv = OP("vector", lambda e, p_=ps, oo=o, hf=half, q_=q0e:
                                            e.tensor_mul(probsT[:, p_ * 1024 + hf * 512 + q_:p_ * 1024 + hf * 512 + 512],
                                                         probsT[:, p_ * 1024 + hf * 512 + q_:p_ * 1024 + hf * 512 + 512],
                                                         masks2[:, oo * 1024 + hf * 512 + q_:oo * 1024 + hf * 512 + 512]), inc="dve")
                                    gate = ("dve", gv)
                                gates.append(gate)
                            pend.append((hh, pvb_, ps, pi, gates[0], gates[1]))
                        if pi == 0:
                            emit_norm()
                    emit_pv()
                    emit_pv()
                    Kh_user[ks0] = pv_done[(h1, pairs - 1)]
                    Kh_user[ks0 + 1] = pv_done[(h2, pairs - 1)]
                    if g < 4:
                        if 2 * hp + 5 < p2_h:
                            p2_load_Kh(g, 2 * hp + 4)
                            p2_load_Kh(g, 2 * hp + 5)
                    elif hp + 1 < p2_h // 2:
                        p2_load_Kh(g, h2 + 1)
                        p2_load_Kh(g, h2 + 2)
                    pending_norm.append((h1, pE, pv_done[(h1, pairs - 1)]))
                    pending_norm.append((h2, pF, pv_done[(h2, pairs - 1)]))
                    if pend_wo:
                        pend_wo.pop(0)()
                emit_norm()
                if pend_wo:
                    pend_wo.pop(0)()
                qTg_user = CNT["pe"]

                if g + 1 < p2_g:
                    emit_qproj(g + 1)
                    for h0 in range(4 if g + 1 < 4 else 2):
                        p2_load_Kh(g + 1, h0)

                def mk_wo_closures(gw, nd0, nd1):
                    evs = {}

                    def stageA(dc):
                        hoff = (gw % 2) * 4096
                        bs = 2 * (dc % 2)
                        if dc == 0:
                            WD("tensor", "wo")
                        W("tensor", "dve", nd0)
                        W("tensor", "dve", nd1)
                        WR("tensor", half_free[bs])
                        mwo = None
                        for ec in range(NDC):
                            mwo = MM(HS[bs], woTb[:, ec * 1024 + dc * 128: ec * 1024 + dc * 128 + 128],
                                     headsT[:, hoff + ec * 512: hoff + (ec + 1) * 512],
                                     start=(ec == 0), stop=(ec == 7), inc="pe" if ec == 7 else None)
                        headsT_user[gw % 2] = mwo
                        W("vector", "pe", mwo)
                        W("vector", "pe", ao_user[dc % 2])
                        ev2 = OP("vector", lambda e, sl=dc % 2, b=HS[bs]:
                                 e.tensor_copy(aob[:, sl * 512: sl * 512 + 512], b), inc="dve")
                        half_free[bs] = ("dve", ev2)
                        evs[dc] = ev2

                    def stageB(dc):
                        nonlocal pG_free, xing_user
                        W("tensor", "dve", evs[dc])
                        WR("tensor", pG_free)
                        ttr = None
                        for j in range(4):
                            ttr = TRx(pG[:, j * 128:(j + 1) * 128],
                                      aob[:, (dc % 2) * 512 + j * 128: (dc % 2) * 512 + j * 128 + 128],
                                      idnf[:, :], inc="pe" if j == 3 else None)
                        ao_user[dc % 2] = ttr
                        W("vector", "pe", ttr)
                        if dc == 0:
                            WD("vector", "xi")
                            WD("vector", "x2s")
                        adds_done = OP("vector", lambda e, d=dc:
                                       e.tensor_add(bass.AP(x2b, d * 128, [[4096, 128], [1024, 4], [1, 128]]),
                                                    bass.AP(pG, 0, [[512, 128], [128, 4], [1, 128]]),
                                                    bass.AP(xing, d * 128, [[4096, 128], [1024, 4], [1, 128]])),
                                       inc="dve")
                        pG_free = ("dve", adds_done)
                        if dc == NDC - 1:
                            xing_user = adds_done
                            W("sync", "dve", adds_done)
                            DMA("sync", "x2s",
                                bass.AP(x2_d, gw * 512 * D, [[D, 128], [128 * D, 4], [1, 1024]]),
                                bass.AP(x2b, 0, [[4096, 128], [1024, 4], [1, 1024]]))
                            if gw + 1 < p2_g:
                                WD("sync", "x2s")
                                DMA("sync", "xi",
                                    bass.AP(xing, 0, [[4096, 128], [1024, 4], [1, 1024]]),
                                    bass.AP(x_d, (gw + 1) * 512 * D, [[D, 128], [128 * D, 4], [1, 1024]]))

                    def mk(dc):
                        def c():
                            if dc > 0:
                                stageB(dc - 1)
                            stageA(dc)
                        return c
                    return [mk(dc) for dc in range(NDC)] + [lambda: stageB(NDC - 1)]

                pend_wo.extend(mk_wo_closures(g, norm_done[0], norm_done[1]))
                if g + 1 == p2_g:
                    while pend_wo:
                        pend_wo.pop(0)()

            quiesce()
            flush_block()
        S12.close()

        with contextlib.ExitStack() as S3:
            x2in = S3.enter_context(nc.sbuf_tensor("x2in", [128, 2 * 4096], F32))
            xsq3 = S3.enter_context(nc.sbuf_tensor("xsq3", [128, 1024], F32))
            xlt3 = S3.enter_context(nc.sbuf_tensor("xlt3", [128, 4096], BF16))
            xl2T = S3.enter_context(nc.sbuf_tensor("xl2T", [128, 2 * 4096], BF16))
            hT = S3.enter_context(nc.sbuf_tensor("hT", [128, NHC * 512], BF16))
            w1b = S3.enter_context(nc.sbuf_tensor("w1b", [128, 4 * 1024], BF16))
            w2b = S3.enter_context(nc.sbuf_tensor("w2b", [128, 4 * 4096], BF16))
            a1r1b = S3.enter_context(nc.sbuf_tensor("a1r1b", [128, NDC * 27], BF16))
            a2r2b = S3.enter_context(nc.sbuf_tensor("a2r2b", [128, NHC * 27], BF16))
            b1ab = S3.enter_context(nc.sbuf_tensor("b1ab", [24, FH], BF16))
            b2ab = S3.enter_context(nc.sbuf_tensor("b2ab", [24, D], BF16))
            b1cs = S3.enter_context(nc.sbuf_tensor("b1cs", [128, NHC], F32))
            b2cs = S3.enter_context(nc.sbuf_tensor("b2cs", [128, NDC], F32))
            zg1s = S3.enter_context(nc.sbuf_tensor("zg1s", [27, 512], F32))
            zts = S3.enter_context(nc.sbuf_tensor("zts", [128, 108], F32))
            gex = S3.enter_context(nc.sbuf_tensor("gex", [128, 12], F32))
            zgt = S3.enter_context(nc.sbuf_tensor("zgt", [128, 96], BF16))
            zg1T = S3.enter_context(nc.sbuf_tensor("zg1T", [24, 512], BF16))
            zg2T = S3.enter_context(nc.sbuf_tensor("zg2T", [24, 512], BF16))
            aob3 = S3.enter_context(nc.sbuf_tensor("aob3", [128, 2 * 512], F32))
            outb = S3.enter_context(nc.sbuf_tensor("outb", [128, 4096], F32))

            DMA("gpsimd", "sm", a1r1b.ap(), a1r1_d.ap())
            DMA("gpsimd", "sm", a2r2b.ap(), a2r2_d.ap())
            DMA("gpsimd", "sm", b1ab.ap(), b1a_d.ap())
            DMA("gpsimd", "sm", b2ab.ap(), b2a_d.ap())
            DMA("sync", "smc", b1cs.ap(), b1c_d.ap())
            DMA("sync", "smc", b2cs.ap(), b2c_d.ap())
            WD("scalar", "smc")
            WD("vector", "smc")

            x2_user = [NONE, NONE]
            x2_user_a = [0, 0]
            w1_user = [0, 0, 0, 0]
            w2_user = [0, 0, 0, 0]
            hT_user = 0
            xl2T_user = [0, 0]
            relu_done = [0, 0]
            obank_free = [0, 0]
            ao3_user = [0, 0]
            zgt_seq = 0

            def p3_load_x2(sc):
                slot = sc % 2
                WR("sync", x2_user[slot])
                W("sync", "act", x2_user_a[slot])
                DMA("sync", "x2l%d" % slot,
                    bass.AP(x2in, slot * 4096, [[2 * 4096, 128], [1024, 4], [1, 1024]]),
                    bass.AP(x2_d, sc * 512 * D, [[D, 128], [128 * D, 4], [1, 1024]]))

            def p3_load_w1(flat):
                sc_, hc_ = divmod(flat, NHC)
                if sc_ >= p3_sc:
                    return
                slot = flat % 4
                W("gpsimd", "pe", w1_user[slot])
                DMA("gpsimd", "w1%d" % slot,
                    bass.AP(w1b, slot * 1024, [[4 * 1024, 128], [1, 1024]]),
                    bass.AP(w1bf_d, hc_ * 128 * 1024, [[1024, 128], [1, 1024]]))

            def p3_load_w2(f):
                if f >= p3_sc * NDC:
                    return
                dc_ = f % NDC
                slot = f % 4
                W("scalar", "pe", w2_user[slot])
                DMA("scalar", "w2%d" % slot,
                    bass.AP(w2b, slot * 4096, [[4 * 4096, 128], [1, 4096]]),
                    bass.AP(w2bf_d, dc_ * 128 * 4096, [[4096, 128], [1, 4096]]))

            p3_load_x2(0)
            if p3_sc > 1:
                p3_load_x2(1)
            for f in range(min(4, p3_sc * NHC)):
                p3_load_w1(f)

            def gates_and_zg(src_T_done):
                nonlocal pG_free, pF_free
                ev3 = ACT(zg1s[:, :], pG[0:27, :], AF.Copy)
                pG_free = ("act", ev3)
                W("tensor", "act", ev3)
                WR("tensor", pF_free)
                tz = None
                for j in range(4):
                    tz = TRx(pF[:, j * 128: j * 128 + 27], zg1s[:, j * 128:(j + 1) * 128],
                             idnf[0:27, 0:27], inc="pe" if j == 3 else None)
                W("vector", "pe", tz)
                dzt = OP("vector", lambda e,
                         i=bass.AP(pF, 0, [[512, 128], [128, 4], [1, 27]]),
                         o=bass.AP(zts, 0, [[108, 128], [27, 4], [1, 27]]):
                         e.tensor_copy(o, i), inc="dve")
                pF_free = ("dve", dzt)
                last = dzt
                for sub in range(4):
                    W("vector", "dve", last)
                    dn = OP("vector", lambda e, s=sub:
                            e.reduce_max(statc(C_NGMX), zts[:, s * 27 + 24: s * 27 + 27], AX.X, negate=True), inc="dve")
                    W("scalar", "dve", dn)
                    ag = ACT(gex[:, sub * 3:(sub + 1) * 3], zts[:, sub * 27 + 24: sub * 27 + 27],
                             AF.Exp, bias=statc(C_NGMX))
                    W("vector", "act", ag)
                    ds = OP("vector", lambda e, s=sub:
                            e.reduce_sum(statc(C_GSUM), gex[:, s * 3:(s + 1) * 3], AX.X), inc="dve")
                    W("vector", "dve", ds)
                    dr = OP("vector", lambda e: e.reciprocal(statc(C_GSC), statc(C_GSUM)), inc="dve")
                    W("vector", "dve", dr)
                    dsc = OP("vector", lambda e: e.tensor_scalar(statc(C_GSC), statc(C_GSC), 0.125, None, op0=ALU.mult), inc="dve")
                    W("vector", "dve", dsc)
                    for ee in range(3):
                        last = OP("vector", lambda e, s=sub, x=ee:
                                  e.tensor_scalar(zgt[:, s * 24 + x * 8: s * 24 + x * 8 + 8],
                                                  zts[:, s * 27 + x * 8: s * 27 + x * 8 + 8],
                                                  gex[:, s * 3 + x: s * 3 + x + 1],
                                                  statc(C_GSC), op0=ALU.mult, op1=ALU.mult), inc="dve")
                return last

            def zg_transpose_out(dst, dzgt):
                nonlocal pT_free
                W("tensor", "dve", dzgt)
                WR("tensor", pT_free)
                tz2 = None
                for j in range(4):
                    tz2 = TRx(pT[0:24, j * 128:(j + 1) * 128], zgt[:, j * 24:(j + 1) * 24],
                              idnb[:, :], inc="pe" if j == 3 else None)
                W("vector", "pe", tz2)
                ev = OP("vector", lambda e,
                        i=bass.AP(pT, 0, [[1024, 24], [128, 4], [1, 128]]),
                        o=bass.AP(dst, 0, [[512, 24], [128, 4], [1, 128]]):
                        e.tensor_copy(o, i), inc="dve")
                pT_free = ("dve", ev)
                return ev

            p3_tr_last = [0]
            xl2T_ready = [0, 0]

            ln2_dxls = {}

            def emit_ln2A(sc, sub):
                slot = sc % 2
                if sub == 0:
                    WD("vector", "x2l%d" % slot)
                    WD("scalar", "x2l%d" % slot)
                    W("vector", "pe", p3_tr_last[0])
                    ln2_dxls[sc] = []
                xi = bass.AP(x2in, slot * 4096 + sub * 1024, [[2 * 4096, 128], [1, 1024]])
                dxl, asq = ln_chain("vector", "scalar", xi, xsq3,
                                    xlt3[:, sub * 1024:(sub + 1) * 1024])
                ln2_dxls[sc].append(dxl)
                if sub == 3:
                    x2_user_a[slot] = asq

            def emit_ln2B(sc):
                nonlocal pT_free
                slot = sc % 2
                dxls = ln2_dxls.pop(sc)
                for sub in range(4):
                    W("tensor", "dve", dxls[sub])
                    WR("tensor", pT_free)
                    tr = None
                    for dc in range(NDC):
                        tr = TRx(pT[:, dc * 128:(dc + 1) * 128],
                                 xlt3[:, sub * 1024 + dc * 128: sub * 1024 + dc * 128 + 128],
                                 idnb[:, :], inc="pe" if dc == 7 else None)
                    W("vector", "pe", tr)
                    if sub == 0:
                        W("vector", "pe", xl2T_user[slot])
                    ev = OP("vector", lambda e,
                            i=bass.AP(pT, 0, [[1024, 128], [128, NDC], [1, 128]]),
                            o=bass.AP(xl2T, slot * 4096 + sub * 128, [[2 * 4096, 128], [512, NDC], [1, 128]]):
                            e.tensor_copy(o, i), inc="dve")
                    pT_free = ("dve", ev)
                p3_tr_last[0] = tr
                xl2T_ready[slot] = ev

            for _sub in range(4):
                emit_ln2A(0, _sub)
            emit_ln2B(0)
            z1_res = {}

            def emit_z1(zc):
                zslot = zc % 2
                WD("tensor", "sm")
                W("tensor", "dve", xl2T_ready[zslot])
                WR("tensor", pG_free)
                mz = None
                for dc in range(NDC):
                    mz = MM(pG[0:27, :], a1r1b[:, dc * 27:(dc + 1) * 27],
                            bass.AP(xl2T, zslot * 4096 + dc * 512, [[2 * 4096, 128], [1, 512]]),
                            start=(dc == 0), stop=(dc == 7), inc="pe" if dc == 7 else None)
                W("scalar", "pe", mz)
                z1_res[zc] = gates_and_zg(mz)

            emit_z1(0)
            for sc in range(p3_sc):
                slot = sc % 2
                dzgt = z1_res.pop(sc)
                dzg1T = None

                if sc > 0:
                    W("scalar", "pe", hT_user)
                for hc in range(NHC):
                    flat = sc * NHC + hc
                    wslot = flat % 4
                    bs = 2 * (hc % 2)
                    WD("tensor", "w1%d" % wslot)
                    WR("tensor", half_free[bs])
                    for dc in range(NDC):
                        MM(HS[bs], w1b[:, wslot * 1024 + dc * 128: wslot * 1024 + dc * 128 + 128],
                           bass.AP(xl2T, slot * 4096 + dc * 512, [[2 * 4096, 128], [1, 512]]),
                           start=(dc == 0), stop=False, inc=None)
                    if hc == 0:
                        dzg1T = zg_transpose_out(zg1T, dzgt)
                        W("tensor", "dve", dzg1T)
                    mh = MM(HS[bs], b1ab[0:24, hc * 128:(hc + 1) * 128], zg1T[0:24, :],
                            start=False, stop=True, inc="pe")
                    w1_user[wslot] = mh
                    p3_load_w1(flat + 4)
                    W("scalar", "pe", mh)
                    rl = ACT(hT[:, hc * 512:(hc + 1) * 512], HS[bs], AF.Relu,
                             bias=b1cs[:, hc:hc + 1])
                    half_free[bs] = ("act", rl)
                    relu_done[hc % 2] = rl
                    if hc in (8, 11, 14, 17) and sc + 1 < p3_sc:
                        emit_ln2A(sc + 1, (hc - 8) // 3)
                    if hc == 15:
                        W("tensor", "act", rl)
                        WR("tensor", pG_free)
                        for hz in range(NHC // 2):
                            MM(pG[0:27, :], a2r2b[:, hz * 27:(hz + 1) * 27],
                               hT[:, hz * 512:(hz + 1) * 512],
                               start=(hz == 0), stop=False, inc=None)
                xl2T_user[slot] = CNT["pe"]
                relu_all = CNT["act"]

                if sc == 0:
                    p3_load_w2(0)
                    p3_load_w2(1)
                    p3_load_w2(2)
                    p3_load_w2(3)
                if sc + 1 < p3_sc:
                    emit_ln2B(sc + 1)

                W("tensor", "act", relu_all)
                mz2 = None
                for hc in range(NHC // 2, NHC):
                    mz2 = MM(pG[0:27, :], a2r2b[:, hc * 27:(hc + 1) * 27],
                             hT[:, hc * 512:(hc + 1) * 512],
                             start=False, stop=(hc == NHC - 1), inc="pe" if hc == NHC - 1 else None)
                W("scalar", "pe", mz2)
                dzgt2 = gates_and_zg(mz2)
                dzg2T = None

                pend_tr = []

                def emit_otr():
                    nonlocal pF_free
                    if not pend_tr:
                        return None
                    d_, dev_ = pend_tr.pop(0)
                    W("tensor", "dve", dev_)
                    WR("tensor", pF_free)
                    to = None
                    for j in range(4):
                        to = TRx(pF[:, j * 128:(j + 1) * 128],
                                 aob3[:, (d_ % 2) * 512 + j * 128: (d_ % 2) * 512 + j * 128 + 128],
                                 idnf[:, :], inc="pe" if j == 3 else None)
                    ao3_user[d_ % 2] = to
                    W("vector", "pe", to)
                    if d_ == 0:
                        WD("vector", "o")
                    adds3 = OP("vector", lambda e, d=d_, sl=slot:
                               e.tensor_add(bass.AP(outb, d * 128, [[4096, 128], [1024, 4], [1, 128]]),
                                            bass.AP(pF, 0, [[512, 128], [128, 4], [1, 128]]),
                                            bass.AP(x2in, sl * 4096 + d * 128, [[2 * 4096, 128], [1024, 4], [1, 128]])),
                               inc="dve")
                    pF_free = ("dve", adds3)
                    return adds3

                for dc in range(NDC):
                    w2slot = (sc * NDC + dc) % 4
                    bs = 2 * (dc % 2) + 1
                    WD("tensor", "w2%d" % w2slot)
                    WR("tensor", half_free[bs])
                    for hc in range(NHC):
                        MM(HS[bs], w2b[:, w2slot * 4096 + hc * 128: w2slot * 4096 + hc * 128 + 128],
                           hT[:, hc * 512:(hc + 1) * 512],
                           start=(hc == 0), stop=False, inc=None)
                    if dc == 0:
                        dzg2T = zg_transpose_out(zg2T, dzgt2)
                        W("tensor", "dve", dzg2T)
                    mo = MM(HS[bs], b2ab[0:24, dc * 128:(dc + 1) * 128], zg2T[0:24, :],
                            start=False, stop=True, inc="pe")
                    w2_user[w2slot] = mo
                    p3_load_w2(sc * NDC + dc + 4)
                    if dc == 4 and sc + 1 < p3_sc:
                        emit_z1(sc + 1)
                    W("vector", "pe", mo)
                    W("vector", "pe", ao3_user[dc % 2])
                    dev = OP("vector", lambda e, d=dc, b=HS[bs], sl=dc % 2:
                             e.tensor_scalar(aob3[:, sl * 512: sl * 512 + 512], b,
                                             b2cs[:, d:d + 1], None, op0=ALU.add), inc="dve")
                    half_free[bs] = ("dve", dev)
                    obank_free[dc % 2] = dev
                    emit_otr()
                    pend_tr.append((dc, dev))
                adds3 = emit_otr()
                hT_user = CNT["pe"]
                x2_user[slot] = ("dve", adds3)
                W("sync", "dve", adds3)
                DMA("sync", "o",
                    bass.AP(y_d, sc * 512 * D, [[D, 128], [128 * D, 4], [1, 1024]]),
                    bass.AP(outb, 0, [[4096, 128], [1024, 4], [1, 1024]]))
                if sc + 2 < p3_sc:
                    p3_load_x2(sc + 2)

            quiesce()
            flush_block()

    return nc


def _host_prep(ins):
    f32 = np.float32
    x = np.ascontiguousarray(np.asarray(ins["L0_attn_x"], dtype=f32))
    wqkv = np.asarray(ins["L0_attn_W_qkv"], dtype=f32)
    wo = np.asarray(ins["L0_attn_W_o"], dtype=f32)
    w1 = np.asarray(ins["L0_ffn_W1"], dtype=f32)
    b1 = np.asarray(ins["L0_ffn_b1"], dtype=f32)
    a1 = np.asarray(ins["L0_ffn_A1"], dtype=f32)
    B1 = np.asarray(ins["L0_ffn_B1"], dtype=f32)
    r1 = np.asarray(ins["L0_ffn_R1"], dtype=f32)
    w2 = np.asarray(ins["L0_ffn_W2"], dtype=f32)
    b2 = np.asarray(ins["L0_ffn_b2"], dtype=f32)
    a2 = np.asarray(ins["L0_ffn_A2"], dtype=f32)
    B2 = np.asarray(ins["L0_ffn_B2"], dtype=f32)
    r2 = np.asarray(ins["L0_ffn_R2"], dtype=f32)

    d = {}
    d["wqkvT"] = np.ascontiguousarray(wqkv.T)
    d["woT"] = np.ascontiguousarray(wo.T)
    d["w1r"] = np.ascontiguousarray(
        w1.reshape(8, 128, 32, 128).transpose(2, 1, 0, 3)).reshape(32, 128, 1024)
    d["w2r"] = np.ascontiguousarray(
        w2.reshape(32, 128, 8, 128).transpose(2, 1, 0, 3)).reshape(8, 128, 4096)
    a1all = a1.transpose(1, 0, 2).reshape(1024, 24)
    a2all = a2.transpose(1, 0, 2).reshape(4096, 24)
    a1r1 = np.zeros((128, 8 * 27), f32)
    for dc in range(8):
        a1r1[:, dc * 27:dc * 27 + 24] = a1all[dc * 128:(dc + 1) * 128, :]
        a1r1[:, dc * 27 + 24:dc * 27 + 27] = r1[dc * 128:(dc + 1) * 128, :]
    a2r2 = np.zeros((128, 32 * 27), f32)
    for hc in range(32):
        a2r2[:, hc * 27:hc * 27 + 24] = a2all[hc * 128:(hc + 1) * 128, :]
        a2r2[:, hc * 27 + 24:hc * 27 + 27] = r2[hc * 128:(hc + 1) * 128, :]
    d["a1r1"] = a1r1
    d["a2r2"] = a2r2
    d["b1a"] = np.ascontiguousarray(B1.reshape(24, 4096))
    d["b2a"] = np.ascontiguousarray(B2.reshape(24, 1024))
    d["b1c"] = np.ascontiguousarray(b1.reshape(32, 128).T)
    d["b2c"] = np.ascontiguousarray(b2.reshape(8, 128).T)
    d["idnb"] = np.eye(128, dtype=ml_dtypes.bfloat16)
    d["idnf"] = np.eye(128, dtype=f32)
    o2 = np.zeros((128, 2), f32)
    o2[0:64, 0] = 1.0
    o2[64:128, 1] = 1.0
    d["ones2"] = o2.astype(ml_dtypes.bfloat16)
    d["ones64r"] = np.ones((1, 64), f32)
    d["ones128c"] = np.ones((128, 1), ml_dtypes.bfloat16)
    d["ones128r"] = np.ones((1, 128), f32)
    kr = np.arange(128)[:, None]
    qc = np.arange(512)[None, :]
    m = np.concatenate([(qc >= kr + 128 * j).astype(f32) for j in range(4)], axis=1)
    d["masks"] = m.astype(ml_dtypes.bfloat16)
    return x, d


def kernel(**ins):
    x, shared = _host_prep(ins)
    nc = build_nc()
    in_maps = []
    for b in range(8):
        m = dict(shared)
        m["x"] = np.ascontiguousarray(x[b])
        in_maps.append(m)
    res = run_bass_kernel_spmd(nc, in_maps, core_ids=list(range(8)))
    return np.stack([np.asarray(res.results[b]["y"]) for b in range(8)], axis=0)
